# version 15
# baseline (speedup 1.0000x reference)
import numpy as np
import contextlib
import concourse.bass as bass
import concourse.mybir as mybir
from concourse.bass_utils import run_bass_kernel_spmd
from concourse.alu_op_type import AluOpType as OP

F32 = mybir.dt.float32
BF16 = mybir.dt.bfloat16
AF = mybir.ActivationFunctionType

NCORES = 8
D = 1024
DC = D // 128
H = 4096
HC = H // 128
T = 512
NT = 2048 // T
SUB = 128
NSUB = T // SUB
OG = 4
EPS_LN = 1e-5
EPS_REF = 1e-5


class Sem:
    def __init__(self, handle):
        self.h = handle
        self.val = 0

    def inc(self, n=1):
        self.val += n
        return self.val


def build(phase_limit=3):
    nc = bass.Bass("TRN2", target_bir_lowering=False)

    def din(name, shape):
        return nc.dram_tensor(name, list(shape), F32, kind="ExternalInput").ap()

    xT = din("xT", (D, 2048))
    w1T = din("w1T", (8, 128, DC * 512)); w2T = din("w2T", (8, 128, DC * 512))
    h1T = din("h1T", (8, 128, HC * 512)); h2T = din("h2T", (8, 128, HC * 512))
    w3T = din("w3T", (2, 128, HC * 512))
    breT = din("breT", (128, DC * D)); bimT = din("bimT", (128, DC * D))
    creT = din("creT", (128, DC * D)); cimT = din("cimT", (128, DC * D))
    s1pc = din("s1pc", (128, HC)); s2pc = din("s2pc", (128, HC))
    h1spc = din("h1spc", (128, HC)); h2spc = din("h2spc", (128, HC))
    w3spc = din("w3spc", (128, DC)); dskpc = din("dskpc", (128, DC))
    nulpc = din("nulpc", (128, DC)); thlpc = din("thlpc", (128, DC))
    outT = nc.dram_tensor("outT", [D, 2048], F32, kind="ExternalOutput").ap()

    def dscr(name, shape):
        return nc.dram_tensor(name, list(shape), BF16, kind="Internal").ap()

    w1S = dscr("w1S", (8, 128, DC * 512)); w2S = dscr("w2S", (8, 128, DC * 512))
    h1S = dscr("h1S", (8, 128, HC * 512)); h2S = dscr("h2S", (8, 128, HC * 512))
    w3S = dscr("w3S", (2, 128, HC * 512))
    bhre = dscr("bhre", (128, DC * D)); bhim = dscr("bhim", (128, DC * D))
    blre = dscr("blre", (128, DC * D)); blim = dscr("blim", (128, DC * D))
    chre = dscr("chre", (128, DC * D)); chim = dscr("chim", (128, DC * D))
    clre = dscr("clre", (128, DC * D)); clim = dscr("clim", (128, DC * D))

    ctx = contextlib.ExitStack()

    L = {}
    _off = [0]

    def reg(name, nbytes):
        L[name] = _off[0]
        _off[0] += (nbytes + 31) // 32 * 32

    reg("x_t", DC * T * 4)
    reg("xn_t", DC * T * 2)
    reg("xn_lo", DC * T * 2)
    reg("tbig", DC * T * 4)
    reg("A_re", DC * SUB * 4); reg("A_im", DC * SUB * 4)
    reg("Rn_re", DC * SUB * 4); reg("Rn_im", DC * SUB * 4)
    reg("S_re", T * 4); reg("S_im", T * 4)
    reg("ones_t", SUB * 4)
    reg("consts", 64 * 4)
    reg("vecs", 192 * 4)
    reg("coef", 224 * 4)
    reg("prep", 8192)
    reg("u_t", HC * T * 2)
    reg("g_t", HC * T * 2)
    reg("wpanA", HC * T * 2)
    reg("wpanB", HC * T * 2)
    total = _off[0]
    assert total <= 212992, f"SBUF over budget: {total}"

    pool_h = ctx.enter_context(
        nc.sbuf_tensor("pool", [128, total], mybir.dt.uint8))
    pool_base = nc.lookup_mloc(pool_h).addr

    def alias(name, shape, dtype, at, extra=0):
        vt = nc.virtual_sbuf_tensor(name, list(shape), dtype)
        nc._tensor(vt.name, list(shape), dtype, type="SB")
        mloc = nc.lookup_mloc(vt)
        mloc.allocated = True
        mloc.addr = pool_base + L[at] + extra
        return vt

    x_t = alias("x_t", [128, DC * T], F32, "x_t")
    xn_t = alias("xn_t", [128, DC * T], BF16, "xn_t")
    xn_lo = alias("xn_lo", [128, DC * T], BF16, "xn_lo")
    tbig = alias("tbig", [128, DC * T], F32, "tbig")
    A_re = alias("A_re", [128, DC * SUB], F32, "A_re")
    A_im = alias("A_im", [128, DC * SUB], F32, "A_im")
    Rn_re = alias("Rn_re", [128, DC * SUB], F32, "Rn_re")
    Rn_im = alias("Rn_im", [128, DC * SUB], F32, "Rn_im")
    S_re = alias("S_re", [128, T], F32, "S_re")
    S_im = alias("S_im", [128, T], F32, "S_im")
    ones_t = alias("ones_t", [128, SUB], F32, "ones_t")
    consts = alias("consts", [128, 64], F32, "consts")
    vecs = alias("vecs", [128, 192], F32, "vecs")
    coef = alias("coef", [128, 224], F32, "coef")
    prep_f = [alias("prep_f0", [128, 512], F32, "prep"),
              alias("prep_f1", [128, 512], F32, "prep", 4096)]
    prep_hi = [alias("prep_h0", [128, 512], BF16, "prep", 2048),
               alias("prep_h1", [128, 512], BF16, "prep", 6144)]
    prep_lo = [alias("prep_l0", [128, 512], BF16, "prep", 3072),
               alias("prep_l1", [128, 512], BF16, "prep", 7168)]
    u_t = alias("u_t", [128, HC * T], BF16, "u_t")
    g_t = alias("g_t", [128, HC * T], BF16, "g_t")
    wpan = [alias("wpanA", [128, HC * T], BF16, "wpanA"),
            alias("wpanB", [128, HC * T], BF16, "wpanB")]
    bh_re = alias("bh_re", [128, DC * D], BF16, "u_t")
    bh_im = alias("bh_im", [128, DC * D], BF16, "u_t", DC * D * 2)
    bl_re = alias("bl_re", [128, DC * D], BF16, "g_t")
    bl_im = alias("bl_im", [128, DC * D], BF16, "g_t", DC * D * 2)
    cp_re = alias("cp_re", [128, DC * D], BF16, "wpanA")
    cp_im = alias("cp_im", [128, DC * D], BF16, "wpanA", DC * D * 2)
    hsrh = alias("hsrh", [128, DC * T], BF16, "wpanB")
    hsrl = alias("hsrl", [128, DC * T], BF16, "wpanB", DC * T * 2)
    hsmh = alias("hsmh", [128, DC * T], BF16, "wpanB", DC * T * 4)
    hsml = alias("hsml", [128, DC * T], BF16, "wpanB", DC * T * 6)

    K_ONECOL, K_PIHALF, K_EPSLN, K_FENCE = 0, 1, 2, 3
    V_S1, V_S2, V_H1S, V_H2S = 0, 32, 64, 96
    V_W3S, V_HALFW3S, V_DSK, V_NUL, V_THL = 128, 136, 144, 152, 160
    C_NU, C_TH, C_EMN, C_EPN = 0, 8, 16, 24
    C_SIN, C_COS, C_GAM = 32, 40, 48
    C_LRE, C_LIM, C_QRE, C_QIM = 56, 64, 72, 80
    C_CT1, C_CT2, C_CT3, C_CT4 = 88, 96, 104, 112
    C_IRE, C_IIM, C_WRE, C_WIM = 120, 128, 136, 144
    C_Q1, C_Q2, C_Q3, C_Q4 = 152, 153, 154, 155
    R_NEGMU, R_M2, R_TMP, R_VAR, R_RSTD, R_NMR = 0, 512, 1024, 1536, 2048, 2560
    LADDER = 3072
    SQ0, SQ1 = 3072, 3584
    LQRE, LQIM, LPRE, LPIM = LADDER, LADDER + 64, LADDER + 128, LADDER + 192

    psum = ctx.enter_context(nc.psum_tensor("ps", [128, 4096], F32))

    def bank(i, w=T):
        return psum[:, i * 512:i * 512 + w]

    def bank_row(i, w=T):
        return psum[0:1, i * 512:i * 512 + w]

    def bank_sub(i, s):
        return psum[:, i * 512 + s * SUB:i * 512 + (s + 1) * SUB]

    s_pe = Sem(ctx.enter_context(nc.semaphore("s_pe")))
    s_ve = Sem(ctx.enter_context(nc.semaphore("s_ve")))
    s_ac = Sem(ctx.enter_context(nc.semaphore("s_ac")))
    s_dsy = Sem(ctx.enter_context(nc.semaphore("s_dsy")))
    s_dgp = Sem(ctx.enter_context(nc.semaphore("s_dgp")))
    s_po = Sem(ctx.enter_context(nc.semaphore("s_po")))
    s_bco = Sem(ctx.enter_context(nc.semaphore("s_bco")))
    s_bco2 = Sem(ctx.enter_context(nc.semaphore("s_bco2")))
    s_wA = Sem(ctx.enter_context(nc.semaphore("s_wA")))
    s_wB = Sem(ctx.enter_context(nc.semaphore("s_wB")))
    s_gA = Sem(ctx.enter_context(nc.semaphore("s_gA")))
    s_gB = Sem(ctx.enter_context(nc.semaphore("s_gB")))
    s_st = Sem(ctx.enter_context(nc.semaphore("s_st")))
    s_ld = Sem(ctx.enter_context(nc.semaphore("s_ld")))

    Q = {"pe": [], "ve": [], "ac": [], "gp": [], "sy": []}

    def emit(q, fn):
        Q[q].append(fn)

    def wait(q, sem, val):
        if sem is not None and val > 0:
            emit(q, lambda e, s=sem.h, v=val: e.wait_ge(s, v))

    def waits(q, pairs):
        best = {}
        for p in pairs:
            if p is None:
                continue
            sem, val = p
            if sem is None or val <= 0:
                continue
            k = id(sem.h)
            if k not in best or best[k][1] < val:
                best[k] = (sem, val)
        for sem, val in best.values():
            wait(q, sem, val)

    def vfence():
        emit("ve", lambda e: e.memset(consts[:, K_FENCE:K_FENCE + 1], 0.0)
             .then_inc(s_ve.h, 1))
        wait("ve", s_ve, s_ve.inc(1))

    bank_free = [None] * 8
    wfree = [None, None]
    parity = [0]

    def claim_banks(banks, extra=()):
        waits("pe", [bank_free[i] for i in banks] + list(extra))

    prep_mark = {}

    def emit_casts(stages):
        for name, dst, srcw, nog in stages:
            for og in range(nog):
                emit("gp", lambda e, d=dst[og], s=srcw[og]:
                     e.dma_start(d, s).then_inc(s_dgp.h, 16))
                wait("gp", s_dgp, s_dgp.inc(16))
                prep_mark[(name, og)] = s_dgp.val

    emit_casts((("w1S", w1S, w1T, 8), ("w2S", w2S, w2T, 8),
                ("w3S", w3S, w3T, 2)))
    for name, dst, srcw in (("bhre", bhre, breT), ("bhim", bhim, bimT),
                            ("chre", chre, creT), ("chim", chim, cimT)):
        emit("gp", lambda e, d=dst, s=srcw:
             e.dma_start(d, s).then_inc(s_dgp.h, 16))
        wait("gp", s_dgp, s_dgp.inc(16))
        prep_mark[name] = s_dgp.val

    def ldvec(dst_off, width, src):
        emit("sy", lambda e, d=vecs[:, dst_off:dst_off + width], s=src:
             e.dma_start(d, s).then_inc(s_dsy.h, 16))
        return s_dsy.inc(16)

    ldvec(V_S1, HC, s1pc); ldvec(V_S2, HC, s2pc)
    ldvec(V_H1S, HC, h1spc); ldvec(V_H2S, HC, h2spc)
    ldvec(V_W3S, DC, w3spc); ldvec(V_DSK, DC, dskpc)
    ldvec(V_NUL, DC, nulpc)
    vec_mark = ldvec(V_THL, DC, thlpc)

    xT3 = xT.rearrange("(c p) t -> p c t", p=128)
    outT3 = outT.rearrange("(c p) t -> p c t", p=128)
    x3 = x_t.ap().rearrange("p (c t) -> p c t", t=T)
    xn3 = xn_t.ap().rearrange("p (c t) -> p c t", t=T)
    xl3 = xn_lo.ap().rearrange("p (c t) -> p c t", t=T)
    tb3 = tbig.ap().rearrange("p (c t) -> p c t", t=T)

    def emit_x_load(tile, dep):
        waits("sy", [dep])
        wait("sy", s_dsy, s_dsy.val)
        emit("sy", lambda e, t=tile: e.dma_start(
            x3, xT3[:, :, t * T:(t + 1) * T]).then_inc(s_dsy.h, 16))
        return s_dsy.inc(16)

    x_mark = emit_x_load(0, None)

    emit("ve", lambda e: e.memset(ones_t[:, :], 1.0))
    emit("ve", lambda e: e.memset(consts[:, K_ONECOL:K_ONECOL + 1], 1.0))
    emit("ve", lambda e: e.memset(consts[:, K_PIHALF:K_PIHALF + 1],
                                  float(np.pi / 2)))
    emit("ve", lambda e: e.memset(consts[:, K_EPSLN:K_EPSLN + 1], float(EPS_LN)))
    emit("ve", lambda e: e.memset(coef[:, C_IRE:C_IRE + DC], 0.0))
    emit("ve", lambda e: e.memset(coef[:, C_IIM:C_IIM + DC], 0.0)
         .then_inc(s_ve.h, 1))
    consts_mark = s_ve.inc(1)

    wait("ve", s_dsy, vec_mark)
    emit("ve", lambda e: e.tensor_scalar(
        vecs[:, V_HALFW3S:V_HALFW3S + DC], vecs[:, V_W3S:V_W3S + DC],
        0.5, None, OP.mult).then_inc(s_ve.h, 1))
    s_ve.inc(1)

    def cc(i, w=DC):
        return coef[:, i:i + w]

    def emit_coef_scalars():
        waits("ac", [(s_dsy, vec_mark), (s_ve, consts_mark)])
        emit("ac", lambda e: e.activation(cc(C_NU), vecs[:, V_NUL:V_NUL + DC],
                                          AF.Exp))
        emit("ac", lambda e: e.activation(cc(C_TH), vecs[:, V_THL:V_THL + DC],
                                          AF.Exp).then_inc(s_ac.h, 1))
        wait("ac", s_ac, s_ac.inc(1))
        emit("ac", lambda e: e.activation(cc(C_EMN), cc(C_NU), AF.Exp,
                                          scale=-1.0))
        emit("ac", lambda e: e.activation(cc(C_EPN), cc(C_NU), AF.Exp))
        emit("ac", lambda e: e.activation(cc(C_SIN), cc(C_TH), AF.Sin))
        emit("ac", lambda e: e.activation(cc(C_COS), cc(C_TH), AF.Sin,
                                          scale=-1.0,
                                          bias=consts[:, K_PIHALF:K_PIHALF + 1]))
        emit("ac", lambda e: e.activation(cc(C_CT1), cc(C_NU), AF.Exp,
                                          scale=-2.0).then_inc(s_ac.h, 1))
        m1 = s_ac.inc(1)
        wait("ve", s_ac, m1)
        emit("ve", lambda e: e.tensor_scalar(cc(C_CT2), cc(C_CT1), -1.0,
                                             1.0 + EPS_REF, OP.mult, OP.add)
             .then_inc(s_ve.h, 1))
        m2 = s_ve.inc(1)
        wait("ac", s_ve, m2)
        emit("ac", lambda e: e.activation(cc(C_CT3), cc(C_CT2), AF.Ln)
             .then_inc(s_ac.h, 1))
        wait("ac", s_ac, s_ac.inc(1))
        emit("ac", lambda e: e.activation(cc(C_GAM), cc(C_CT3), AF.Exp,
                                          scale=0.5).then_inc(s_ac.h, 1))
        return s_ac.inc(1)

    def emit_bc_prep():
        chunks = []
        for srcT, hiS, loS in ((breT, bhre, blre), (bimT, bhim, blim),
                               (creT, chre, clre), (cimT, chim, clim)):
            for j in range(16):
                cs = slice(j * 512, (j + 1) * 512)
                chunks.append((srcT, hiS, loS, cs))
        in_marks = [None] * len(chunks)

        def issue_in(i):
            srcT, hiS, loS, cs = chunks[i]
            buf = i % 2
            emit("gp", lambda e, s=srcT[:, cs], b=buf:
                 e.dma_start(prep_f[b][:, :], s).then_inc(s_dgp.h, 16))
            s_dgp.inc(16)
            emit("gp", lambda e, s=hiS[:, cs], b=buf:
                 e.dma_start(prep_hi[b][:, :], s).then_inc(s_dgp.h, 16))
            in_marks[i] = s_dgp.inc(16)

        issue_in(0)
        out_marks = [0] * len(chunks)
        for i in range(len(chunks)):
            srcT, hiS, loS, cs = chunks[i]
            buf = i % 2
            wait("gp", s_dgp, in_marks[i])
            if i + 1 < len(chunks):
                issue_in(i + 1)
            osem = s_bco if buf == 0 else s_bco2
            if i >= 2:
                wait("gp", osem, out_marks[i - 2])
            emit("gp", lambda e, b=buf: e.tensor_tensor(
                prep_lo[b][:, :], prep_f[b][:, :], prep_hi[b][:, :],
                OP.subtract).then_inc(s_po.h, 1))
            wait("gp", s_po, s_po.inc(1))
            emit("gp", lambda e, d=loS[:, cs], b=buf, sm=osem.h:
                 e.dma_start(d, prep_lo[b][:, :]).then_inc(sm, 16))
            out_marks[i] = osem.inc(16)
            if i == 31:
                prep_mark["blo"] = (s_bco.val, s_bco2.val)
        prep_mark["clo"] = (s_bco.val, s_bco2.val)

    def emit_coef_arrays(scal_mark):
        wait("ve", s_ac, scal_mark)
        rows = tbig
        emit("ve", lambda e: e.tensor_tensor(cc(C_LRE), cc(C_EMN), cc(C_COS),
                                             OP.mult))
        emit("ve", lambda e: e.tensor_tensor(cc(C_LIM), cc(C_EMN), cc(C_SIN),
                                             OP.mult))
        emit("ve", lambda e: e.tensor_tensor(cc(C_QRE), cc(C_EPN), cc(C_COS),
                                             OP.mult))
        emit("ve", lambda e: e.scalar_tensor_tensor(cc(C_QIM), cc(C_EPN), -1.0,
                                                    cc(C_SIN), OP.mult,
                                                    OP.mult))
        vfence()
        emit("ve", lambda e: e.tensor_copy(rows[:, LQRE:LQRE + DC], cc(C_QRE)))
        emit("ve", lambda e: e.tensor_copy(rows[:, LQIM:LQIM + DC], cc(C_QIM)))
        emit("ve", lambda e: e.tensor_copy(rows[:, LPRE:LPRE + DC], cc(C_LRE)))
        emit("ve", lambda e: e.tensor_copy(rows[:, LPIM:LPIM + DC], cc(C_LIM)))
        vfence()
        for k in range(7):
            qre = rows[:, LQRE + k * DC:LQRE + (k + 1) * DC]
            qim = rows[:, LQIM + k * DC:LQIM + (k + 1) * DC]
            pre = rows[:, LPRE + k * DC:LPRE + (k + 1) * DC]
            pim = rows[:, LPIM + k * DC:LPIM + (k + 1) * DC]
            emit("ve", lambda e, a=qre: e.tensor_tensor(cc(C_CT1), a, a,
                                                        OP.mult))
            emit("ve", lambda e, a=qim: e.tensor_tensor(cc(C_CT2), a, a,
                                                        OP.mult))
            emit("ve", lambda e, a=pre: e.tensor_tensor(cc(C_CT3), a, a,
                                                        OP.mult))
            emit("ve", lambda e, a=pim: e.tensor_tensor(cc(C_CT4), a, a,
                                                        OP.mult))
            vfence()
            emit("ve", lambda e, a=qre, b=qim,
                 d=rows[:, LQIM + (k + 1) * DC:LQIM + (k + 2) * DC]:
                 e.scalar_tensor_tensor(d, a, 2.0, b, OP.mult, OP.mult))
            emit("ve", lambda e, a=pre, b=pim,
                 d=rows[:, LPIM + (k + 1) * DC:LPIM + (k + 2) * DC]:
                 e.scalar_tensor_tensor(d, a, 2.0, b, OP.mult, OP.mult))
            emit("ve", lambda e,
                 d=rows[:, LQRE + (k + 1) * DC:LQRE + (k + 2) * DC]:
                 e.tensor_tensor(d, cc(C_CT1), cc(C_CT2), OP.subtract))
            emit("ve", lambda e,
                 d=rows[:, LPRE + (k + 1) * DC:LPRE + (k + 2) * DC]:
                 e.tensor_tensor(d, cc(C_CT3), cc(C_CT4), OP.subtract))
            vfence()
        emit("ve", lambda e: e.tensor_copy(
            cc(C_WRE), rows[:, LPRE + 7 * DC:LPRE + 8 * DC]))
        emit("ve", lambda e: e.tensor_copy(
            cc(C_WIM), rows[:, LPIM + 7 * DC:LPIM + 8 * DC]))
        vfence()

        st_re = [tbig[:, c * SUB:(c + 1) * SUB] for c in range(DC)]
        st_im = [tbig[:, 1024 + c * SUB:1024 + (c + 1) * SUB]
                 for c in range(DC)]
        tmpu = [tbig[:, 2048 + i * SUB:2048 + (i + 1) * SUB] for i in range(4)]
        tmpv = [tbig[:, 2560 + i * SUB:2560 + (i + 1) * SUB] for i in range(4)]

        def build_arr(dst_re, dst_im, lre, lim, base_kind):
            for c in range(DC):
                if base_kind == "gamma":
                    emit("ve", lambda e, c=c: e.tensor_copy(
                        st_re[c][:, 0:1], coef[:, C_GAM + c:C_GAM + c + 1]))
                else:
                    emit("ve", lambda e, c=c: e.memset(st_re[c][:, 0:1], -1.0))
                emit("ve", lambda e, c=c: e.memset(st_im[c][:, 0:1], 0.0))
            vfence()
            n = 1
            for k in range(7):
                for g0 in range(0, DC, 4):
                    for i, c in enumerate(range(g0, g0 + 4)):
                        pr = rows[:, lre + k * DC + c:lre + k * DC + c + 1]
                        pi_ = rows[:, lim + k * DC + c:lim + k * DC + c + 1]
                        emit("ve", lambda e, c=c, pi_=pi_, u=tmpu[i], n=n:
                             e.tensor_scalar(u[:, 0:n], st_im[c][:, 0:n],
                                             pi_, None, OP.mult))
                        emit("ve", lambda e, c=c, pr=pr, v=tmpv[i], n=n:
                             e.tensor_scalar(v[:, 0:n], st_im[c][:, 0:n],
                                             pr, None, OP.mult))
                    vfence()
                    for i, c in enumerate(range(g0, g0 + 4)):
                        pr = rows[:, lre + k * DC + c:lre + k * DC + c + 1]
                        pi_ = rows[:, lim + k * DC + c:lim + k * DC + c + 1]
                        emit("ve", lambda e, c=c, pr=pr, u=tmpu[i], n=n:
                             e.scalar_tensor_tensor(
                                 st_re[c][:, n:2 * n], st_re[c][:, 0:n], pr,
                                 u[:, 0:n], OP.mult, OP.subtract))
                        emit("ve", lambda e, c=c, pi_=pi_, v=tmpv[i], n=n:
                             e.scalar_tensor_tensor(
                                 st_im[c][:, n:2 * n], st_re[c][:, 0:n], pi_,
                                 v[:, 0:n], OP.mult, OP.add))
                    vfence()
                n *= 2
            emit("ve", lambda e, d=dst_re: e.tensor_copy(
                d[:, :], tbig[:, 0:DC * SUB]))
            emit("ve", lambda e, d=dst_im: e.tensor_copy(
                d[:, :], tbig[:, 1024:1024 + DC * SUB]))
            vfence()

        build_arr(A_re, A_im, LQRE, LQIM, "gamma")
        build_arr(Rn_re, Rn_im, LPRE, LPIM, "neg1")
        emit("ve", lambda e: e.memset(consts[:, K_FENCE:K_FENCE + 1], 0.0)
             .then_inc(s_ve.h, 1))
        return s_ve.inc(1)

    def row(o, w=T):
        return tbig[0:1, o:o + w]

    def emit_ln(x_dep_pairs, produce_lo, x_dep_h0=None):
        h0 = x_dep_h0 if x_dep_h0 is not None else x_dep_pairs
        claim_banks([0, 1], extra=list(h0) + [(s_ve, consts_mark)])
        for dk in range(DC):
            if dk == 4:
                waits("pe", x_dep_pairs)
            emit("pe", lambda e, dk=dk: e.matmul(
                bank_row(0), consts[:, K_ONECOL:K_ONECOL + 1], x3[:, dk, :],
                start=(dk == 0), stop=(dk == DC - 1)))
        waits("ac", h0)
        sq = [tbig[:, SQ0:SQ0 + T], tbig[:, SQ1:SQ1 + T]]
        sq_ac, pe_sq = [], []
        for dk in range(DC):
            dst = sq[dk % 2]
            if dk == 4:
                waits("ac", x_dep_pairs)
            if dk >= 2:
                wait("ac", s_pe, pe_sq[dk - 2])
            emit("ac", lambda e, dst=dst, dk=dk: e.activation(
                dst, x3[:, dk, :], AF.Square).then_inc(s_ac.h, 1))
            sq_ac.append(s_ac.inc(1))
            wait("pe", s_ac, sq_ac[dk])
            emit("pe", lambda e, src=dst, dk=dk: e.matmul(
                bank_row(1), consts[:, K_ONECOL:K_ONECOL + 1], src,
                start=(dk == 0), stop=(dk == DC - 1)).then_inc(s_pe.h, 1))
            pe_sq.append(s_pe.inc(1))
        stats_mark = pe_sq[-1]

        wait("ve", s_pe, stats_mark)
        emit("ve", lambda e: e.tensor_scalar(row(R_NEGMU), bank_row(0),
                                             -1.0 / D, None, OP.mult))
        emit("ve", lambda e: e.tensor_scalar(row(R_M2), bank_row(1),
                                             1.0 / D, None, OP.mult)
             .then_inc(s_ve.h, 1))
        mu_mark = s_ve.inc(1)
        wait("ac", s_ve, mu_mark)
        emit("ac", lambda e: e.activation(row(R_TMP), row(R_NEGMU), AF.Square)
             .then_inc(s_ac.h, 1))
        sqm_mark = s_ac.inc(1)
        wait("ve", s_ac, sqm_mark)
        emit("ve", lambda e: e.tensor_tensor(row(R_VAR), row(R_M2), row(R_TMP),
                                             OP.subtract).then_inc(s_ve.h, 1))
        var_mark = s_ve.inc(1)
        wait("ac", s_ve, var_mark)
        emit("ac", lambda e: e.activation(row(R_RSTD), row(R_VAR), AF.Ln,
                                          bias=consts[0:1, K_EPSLN:K_EPSLN + 1])
             .then_inc(s_ac.h, 1))
        wait("ac", s_ac, s_ac.inc(1))
        emit("ac", lambda e: e.activation(row(R_RSTD), row(R_RSTD), AF.Exp,
                                          scale=-0.5).then_inc(s_ac.h, 1))
        rstd_mark = s_ac.inc(1)
        wait("ve", s_ac, rstd_mark)
        emit("ve", lambda e: e.tensor_tensor(row(R_NMR), row(R_NEGMU),
                                             row(R_RSTD), OP.mult)
             .then_inc(s_ve.h, 1))
        nmr_mark = s_ve.inc(1)
        claim_banks([2, 3], extra=[(s_ve, nmr_mark)])
        emit("pe", lambda e: e.matmul(bank(2), ones_t[0:1, 0:128], row(R_RSTD),
                                      start=True, stop=True))
        emit("pe", lambda e: e.matmul(bank(3), ones_t[0:1, 0:128], row(R_NMR),
                                      start=True, stop=True)
             .then_inc(s_pe.h, 1))
        bc_mark = s_pe.inc(1)
        wait("ve", s_pe, bc_mark)
        for dk in range(DC):
            emit("ve", lambda e, dk=dk: e.tensor_tensor(
                tb3[:, dk, :], x3[:, dk, :], bank(2), OP.mult))
        vfence()
        for dk in range(DC):
            last = (dk == DC - 1) and not produce_lo
            if last:
                emit("ve", lambda e, dk=dk: e.tensor_tensor(
                    xn3[:, dk, :], tb3[:, dk, :], bank(3), OP.add)
                     .then_inc(s_ve.h, 1))
            else:
                emit("ve", lambda e, dk=dk: e.tensor_tensor(
                    xn3[:, dk, :], tb3[:, dk, :], bank(3), OP.add))
        if not produce_lo:
            apply_mark = s_ve.inc(1)
        else:
            vfence()
            for dk in range(DC):
                emit("ve", lambda e, dk=dk: e.tensor_tensor(
                    tb3[:, dk, :], tb3[:, dk, :], xn3[:, dk, :], OP.subtract))
            vfence()
            for dk in range(DC):
                if dk == DC - 1:
                    emit("ve", lambda e, dk=dk: e.tensor_tensor(
                        xl3[:, dk, :], tb3[:, dk, :], bank(3), OP.add)
                         .then_inc(s_ve.h, 1))
                else:
                    emit("ve", lambda e, dk=dk: e.tensor_tensor(
                        xl3[:, dk, :], tb3[:, dk, :], bank(3), OP.add))
            apply_mark = s_ve.inc(1)
        bank_free[0] = (s_ve, mu_mark)
        bank_free[1] = (s_ve, mu_mark)
        bank_free[2] = (s_ve, apply_mark)
        bank_free[3] = (s_ve, apply_mark)
        return apply_mark

    def og_load_cast(srcF, KC, og, par):
        sem = s_gA if par == 0 else s_gB
        waits("gp", [wfree[par], (sem, sem.val)])
        emit("gp", lambda e, d=wpan[par][:, 0:KC * 512], s=srcF[og], sm=sem.h:
             e.dma_start(d, s).then_inc(sm, 16))
        sem.inc(16)
        return (sem, sem.val)

    def og_load(scrS, KC, og, par, prepname):
        sem = s_wA if par == 0 else s_wB
        waits("sy", [wfree[par], (s_dgp, prep_mark[(prepname, og)]),
                     (sem, sem.val)])
        emit("sy", lambda e, d=wpan[par][:, 0:KC * 512], s=scrS[og], sm=sem.h:
             e.dma_start(d, s).then_inc(sm, 16))
        sem.inc(16)
        return (sem, sem.val)

    def ffn_stage(scrS, prepname, KC, OC, rhs_t, evac, rhs_dep,
                  cast_src=None):
        rhs3 = rhs_t.ap().rearrange("p (k t) -> p k t", t=T)
        marks = []
        for og in range(OC // OG):
            par = parity[0]
            parity[0] ^= 1
            if cast_src is not None:
                dmark = og_load_cast(cast_src, KC, og, par)
            else:
                dmark = og_load(scrS, KC, og, par, prepname)
            banks = [par * 4 + i for i in range(OG)]
            claim_banks(banks, extra=[dmark] + list(rhs_dep))
            for k in range(KC):
                for oc in range(OG):
                    st, sp = (k == 0), (k == KC - 1)
                    w_ap = wpan[par][:, k * 512 + oc * 128:
                                     k * 512 + oc * 128 + 128]
                    if sp and oc == OG - 1:
                        emit("pe", lambda e, w=w_ap, r=rhs3[:, k, :],
                             o=bank(banks[oc]), st=st:
                             e.matmul(o, w, r, start=st, stop=True)
                             .then_inc(s_pe.h, 1))
                    else:
                        emit("pe", lambda e, w=w_ap, r=rhs3[:, k, :],
                             o=bank(banks[oc]), st=st, sp=sp:
                             e.matmul(o, w, r, start=st, stop=sp))
            grp = s_pe.inc(1)
            wfree[par] = (s_pe, grp)
            ev_mark = evac(og, banks, grp)
            for bk in banks:
                bank_free[bk] = ev_mark
            marks.append((grp, ev_mark))
        return marks

    def evac_act(dst_t, func, svec_off):
        dst3 = dst_t.ap().rearrange("p (k t) -> p k t", t=T)

        def f(og, banks, grp):
            wait("ac", s_pe, grp)
            for i, bk in enumerate(banks):
                ocg = og * OG + i
                sc = vecs[:, svec_off + ocg:svec_off + ocg + 1]
                if i == OG - 1:
                    emit("ac", lambda e, bk=bk, ocg=ocg, sc=sc: e.activation(
                        dst3[:, ocg, :], bank(bk), func, scale=sc)
                         .then_inc(s_ac.h, 1))
                else:
                    emit("ac", lambda e, bk=bk, ocg=ocg, sc=sc: e.activation(
                        dst3[:, ocg, :], bank(bk), func, scale=sc))
            return (s_ac, s_ac.inc(1))
        return f

    def evac_mul(dst_t, a_t, svec_off):
        dst3 = dst_t.ap().rearrange("p (k t) -> p k t", t=T)
        a3 = a_t.ap().rearrange("p (k t) -> p k t", t=T)

        def f(og, banks, grp):
            wait("ve", s_pe, grp)
            for i, bk in enumerate(banks):
                ocg = og * OG + i
                sc = vecs[:, svec_off + ocg:svec_off + ocg + 1]
                if i == OG - 1:
                    emit("ve", lambda e, bk=bk, ocg=ocg, sc=sc:
                         e.scalar_tensor_tensor(dst3[:, ocg, :], bank(bk), sc,
                                                a3[:, ocg, :], OP.mult,
                                                OP.mult).then_inc(s_ve.h, 1))
                else:
                    emit("ve", lambda e, bk=bk, ocg=ocg, sc=sc:
                         e.scalar_tensor_tensor(dst3[:, ocg, :], bank(bk), sc,
                                                a3[:, ocg, :], OP.mult,
                                                OP.mult))
            return (s_ve, s_ve.inc(1))
        return f

    def evac_resid(svec_off):
        def f(og, banks, grp):
            wait("ve", s_pe, grp)
            for i, bk in enumerate(banks):
                ocg = og * OG + i
                sc = vecs[:, svec_off + ocg:svec_off + ocg + 1]
                if i == OG - 1:
                    emit("ve", lambda e, bk=bk, ocg=ocg, sc=sc:
                         e.scalar_tensor_tensor(x3[:, ocg, :], bank(bk), sc,
                                                x3[:, ocg, :], OP.mult,
                                                OP.add).then_inc(s_ve.h, 1))
                else:
                    emit("ve", lambda e, bk=bk, ocg=ocg, sc=sc:
                         e.scalar_tensor_tensor(x3[:, ocg, :], bank(bk), sc,
                                                x3[:, ocg, :], OP.mult,
                                                OP.add))
            return (s_ve, s_ve.inc(1))
        return f

    coef_state = {"pending": True, "scal": None, "mark": None, "hcast": []}

    def emit_ffn(xn_mark, tile):
        dep = [(s_ve, xn_mark)]
        m1 = ffn_stage(w1S, "w1S", DC, HC, xn_t,
                       evac_act(g_t, AF.Silu, V_S1), dep)
        a_done = m1[-1][1]
        m2 = ffn_stage(w2S, "w2S", DC, HC, xn_t,
                       evac_mul(u_t, g_t, V_S2), dep + [a_done])
        u_done = m2[-1][1]
        first = tile == 0 and coef_state["pending"]
        if first:
            coef_state["mark"] = emit_coef_arrays(coef_state["scal"])
            coef_state["pending"] = False
        m3 = ffn_stage(h1S, "h1S", HC, HC, u_t,
                       evac_act(g_t, AF.Gelu, V_H1S), [u_done],
                       cast_src=h1T if first else None)
        g_done = m3[-1][1]
        m4 = ffn_stage(h2S, "h2S", HC, HC, g_t,
                       evac_act(u_t, AF.Gelu, V_H2S), [g_done],
                       cast_src=h2T if first else None)
        g2_done = m4[-1][1]
        m5 = ffn_stage(w3S, "w3S", HC, DC, u_t,
                       evac_resid(V_HALFW3S), [g2_done])
        if first:
            emit_bc_prep()
            emit_casts((("h1S", h1S, h1T, 8), ("h2S", h2S, h2T, 8)))
        return m5[-1][1], s_pe.val, [m5[0][1], m5[1][1]], m4[-1][0]

    bh_re3 = bh_re.ap().rearrange("p (k h) -> p k h", h=D)
    bh_im3 = bh_im.ap().rearrange("p (k h) -> p k h", h=D)
    bl_re3 = bl_re.ap().rearrange("p (k h) -> p k h", h=D)
    bl_im3 = bl_im.ap().rearrange("p (k h) -> p k h", h=D)
    cp_re3 = cp_re.ap().rearrange("p (k d) -> p k d", d=D)
    cp_im3 = cp_im.ap().rearrange("p (k d) -> p k d", d=D)
    hsrh3 = hsrh.ap().rearrange("p (c t) -> p c t", t=T)
    hsrl3 = hsrl.ap().rearrange("p (c t) -> p c t", t=T)
    hsmh3 = hsmh.ap().rearrange("p (c t) -> p c t", t=T)
    hsml3 = hsml.ap().rearrange("p (c t) -> p c t", t=T)
    A_re3 = A_re.ap().rearrange("p (c j) -> p c j", j=SUB)
    A_im3 = A_im.ap().rearrange("p (c j) -> p c j", j=SUB)
    Rn_re3 = Rn_re.ap().rearrange("p (c j) -> p c j", j=SUB)
    Rn_im3 = Rn_im.ap().rearrange("p (c j) -> p c j", j=SUB)
    def emit_attn(tile, xn_mark, ffn1_pe_mark, m4_pe_mark):
        waits("sy", [(s_pe, m4_pe_mark), (s_bco, prep_mark["blo"][0]),
                     (s_bco2, prep_mark["blo"][1]), (s_dsy, s_dsy.val)])
        for dt_, s3 in ((bl_re, blre), (bl_im, blim)):
            emit("sy", lambda e, d=dt_, s=s3:
                 e.dma_start(d[:, :], s).then_inc(s_dsy.h, 16))
            s_dsy.inc(16)
        waits("sy", [(s_pe, ffn1_pe_mark), (s_dgp, prep_mark["bhim"]),
                     (s_dsy, s_dsy.val)])
        for dt_, s3 in ((bh_re, bhre), (bh_im, bhim)):
            emit("sy", lambda e, d=dt_, s=s3:
                 e.dma_start(d[:, :], s).then_inc(s_dsy.h, 16))
            s_dsy.inc(16)
        bpan_mark = s_dsy.val
        waits("sy", [wfree[0], wfree[1], (s_dgp, prep_mark["chim"]),
                     (s_dsy, s_dsy.val)])
        for dt_, s3 in ((cp_re, chre), (cp_im, chim)):
            emit("sy", lambda e, d=dt_, s=s3:
                 e.dma_start(d[:, :], s).then_inc(s_dsy.h, 16))
            s_dsy.inc(16)
        chi_mark = s_dsy.val

        hs_marks = []
        for c in range(DC):
            pair = (c % 2) * 2
            extra = [(s_dsy, bpan_mark), (s_ve, xn_mark)]
            if tile == 0 and c == 0 and coef_state["mark"] is not None:
                extra.append((s_ve, coef_state["mark"]))
            claim_banks([pair, pair + 1], extra=extra)
            passes = [(bh_re3, bh_im3, xn3), (bl_re3, bl_im3, xn3),
                      (bh_re3, bh_im3, xl3)]
            for p_idx, (pre3, pim3, rhs3) in enumerate(passes):
                first = (p_idx == 0)
                last = (p_idx == len(passes) - 1)
                for dk in range(DC):
                    emit("pe", lambda e, dk=dk, c=c, p3=pre3, r3=rhs3, pb=pair,
                         st=(first and dk == 0), sp=(last and dk == DC - 1):
                         e.matmul(bank(pb), p3[:, dk, c * 128:(c + 1) * 128],
                                  r3[:, dk, :], start=st, stop=sp))
                for dk in range(DC):
                    st = first and dk == 0
                    sp = last and dk == DC - 1
                    if sp:
                        emit("pe", lambda e, dk=dk, c=c, p3=pim3, r3=rhs3,
                             pb=pair, st=st:
                             e.matmul(bank(pb + 1),
                                      p3[:, dk, c * 128:(c + 1) * 128],
                                      r3[:, dk, :], start=st, stop=True)
                             .then_inc(s_pe.h, 1))
                    else:
                        emit("pe", lambda e, dk=dk, c=c, p3=pim3, r3=rhs3,
                             pb=pair, st=st:
                             e.matmul(bank(pb + 1),
                                      p3[:, dk, c * 128:(c + 1) * 128],
                                      r3[:, dk, :], start=st, stop=False))
            bu_mark = s_pe.inc(1)

            wait("ve", s_pe, bu_mark)
            vfence()
            for s in range(NSUB):
                bre = bank_sub(pair, s)
                bim = bank_sub(pair + 1, s)
                for i, (arr3, b) in enumerate(((A_re3, bre), (A_im3, bim),
                                               (A_im3, bre), (A_re3, bim))):
                    q = (s * 4 + i) * SUB
                    emit("ve", lambda e, c=c, a3=arr3, b=b, q=q:
                         e.tensor_tensor(tbig[:, q:q + SUB], a3[:, c, :], b,
                                         OP.mult))
            vfence()
            bankrel = s_ve.val
            for s in range(NSUB):
                q = s * 4 * SUB
                emit("ve", lambda e, q=q, d=2048 + s * SUB:
                     e.tensor_tensor(tbig[:, d:d + SUB], tbig[:, q:q + SUB],
                                     tbig[:, q + SUB:q + 2 * SUB],
                                     OP.subtract))
                emit("ve", lambda e, q=q, d=2560 + s * SUB:
                     e.tensor_tensor(tbig[:, d:d + SUB],
                                     tbig[:, q + 2 * SUB:q + 3 * SUB],
                                     tbig[:, q + 3 * SUB:q + 4 * SUB],
                                     OP.add))
            vfence()
            for s in range(NSUB):
                cs = slice(s * SUB, (s + 1) * SUB)
                emit("ve", lambda e, c=c, cs=cs, d=2048 + s * SUB:
                     e.tensor_tensor_scan(
                         S_re[:, cs], ones_t[:, :], tbig[:, d:d + SUB],
                         coef[:, C_IRE + c:C_IRE + c + 1], OP.mult, OP.add))
                emit("ve", lambda e, c=c, cs=cs, d=2560 + s * SUB:
                     e.tensor_tensor_scan(
                         S_im[:, cs], ones_t[:, :], tbig[:, d:d + SUB],
                         coef[:, C_IIM + c:C_IIM + c + 1], OP.mult, OP.add))
                vfence()
                send = s * SUB + SUB - 1
                emit("ve", lambda e, c=c, send=send: e.tensor_tensor(
                    coef[:, C_Q1:C_Q1 + 1],
                    coef[:, C_WRE + c:C_WRE + c + 1],
                    S_re[:, send:send + 1], OP.mult))
                emit("ve", lambda e, c=c, send=send: e.tensor_tensor(
                    coef[:, C_Q2:C_Q2 + 1],
                    coef[:, C_WIM + c:C_WIM + c + 1],
                    S_im[:, send:send + 1], OP.mult))
                emit("ve", lambda e, c=c, send=send: e.tensor_tensor(
                    coef[:, C_Q3:C_Q3 + 1],
                    coef[:, C_WRE + c:C_WRE + c + 1],
                    S_im[:, send:send + 1], OP.mult))
                emit("ve", lambda e, c=c, send=send: e.tensor_tensor(
                    coef[:, C_Q4:C_Q4 + 1],
                    coef[:, C_WIM + c:C_WIM + c + 1],
                    S_re[:, send:send + 1], OP.mult))
                vfence()
                emit("ve", lambda e, c=c: e.tensor_tensor(
                    coef[:, C_IRE + c:C_IRE + c + 1],
                    coef[:, C_Q1:C_Q1 + 1], coef[:, C_Q2:C_Q2 + 1],
                    OP.subtract))
                emit("ve", lambda e, c=c: e.tensor_tensor(
                    coef[:, C_IIM + c:C_IIM + c + 1],
                    coef[:, C_Q3:C_Q3 + 1], coef[:, C_Q4:C_Q4 + 1], OP.add))
                vfence()
            for s in range(NSUB):
                cs = slice(s * SUB, (s + 1) * SUB)
                for i, (r3, Sb) in enumerate(((Rn_im3, S_im), (Rn_re3, S_re),
                                              (Rn_re3, S_im), (Rn_im3, S_re))):
                    q = (s * 4 + i) * SUB
                    emit("ve", lambda e, c=c, cs=cs, r3=r3, Sb=Sb, q=q:
                         e.tensor_tensor(tbig[:, q:q + SUB], r3[:, c, :],
                                         Sb[:, cs], OP.mult))
            vfence()
            for s in range(NSUB):
                q = s * 4 * SUB
                emit("ve", lambda e, q=q, d=2048 + s * SUB:
                     e.tensor_tensor(tbig[:, d:d + SUB], tbig[:, q:q + SUB],
                                     tbig[:, q + SUB:q + 2 * SUB],
                                     OP.subtract))
                emit("ve", lambda e, q=q, d=2560 + s * SUB:
                     e.tensor_tensor(tbig[:, d:d + SUB],
                                     tbig[:, q + 2 * SUB:q + 3 * SUB],
                                     tbig[:, q + 3 * SUB:q + 4 * SUB],
                                     OP.add))
            vfence()
            for s in range(NSUB):
                cs = slice(s * SUB, (s + 1) * SUB)
                emit("ve", lambda e, c=c, cs=cs, d=2048 + s * SUB:
                     e.tensor_copy(hsrh3[:, c, cs], tbig[:, d:d + SUB]))
                emit("ve", lambda e, c=c, cs=cs, d=2560 + s * SUB:
                     e.tensor_copy(hsmh3[:, c, cs], tbig[:, d:d + SUB]))
            vfence()
            for s in range(NSUB):
                cs = slice(s * SUB, (s + 1) * SUB)
                emit("ve", lambda e, c=c, cs=cs, d=2048 + s * SUB:
                     e.tensor_tensor(hsrl3[:, c, cs], tbig[:, d:d + SUB],
                                     hsrh3[:, c, cs], OP.subtract))
                last = s == NSUB - 1
                if last:
                    emit("ve", lambda e, c=c, cs=cs, d=2560 + s * SUB:
                         e.tensor_tensor(hsml3[:, c, cs], tbig[:, d:d + SUB],
                                         hsmh3[:, c, cs], OP.subtract)
                         .then_inc(s_ve.h, 1))
                else:
                    emit("ve", lambda e, c=c, cs=cs, d=2560 + s * SUB:
                         e.tensor_tensor(hsml3[:, c, cs], tbig[:, d:d + SUB],
                                         hsmh3[:, c, cs], OP.subtract))
            hs_mark = s_ve.inc(1)
            hs_marks.append(hs_mark)
            bank_free[pair] = (s_ve, bankrel)
            bank_free[pair + 1] = (s_ve, bankrel)

        claim_banks(list(range(8)), extra=[(s_dsy, chi_mark)])
        for c in range(DC):
            wait("pe", s_ve, hs_marks[c])
            for dd in range(DC):
                st = (c == 0)
                emit("pe", lambda e, c=c, dd=dd, st=st: e.matmul(
                    bank(dd), cp_re3[:, c, dd * 128:(dd + 1) * 128],
                    hsrh3[:, c, :], start=st, stop=False))
                emit("pe", lambda e, c=c, dd=dd: e.matmul(
                    bank(dd), cp_im3[:, c, dd * 128:(dd + 1) * 128],
                    hsmh3[:, c, :], start=False, stop=False))
                emit("pe", lambda e, c=c, dd=dd: e.matmul(
                    bank(dd), cp_re3[:, c, dd * 128:(dd + 1) * 128],
                    hsrl3[:, c, :], start=False, stop=False))
                inc_here = (c == DC - 1 and dd == DC - 1)
                if inc_here:
                    emit("pe", lambda e, c=c, dd=dd: e.matmul(
                        bank(dd), cp_im3[:, c, dd * 128:(dd + 1) * 128],
                        hsml3[:, c, :], start=False, stop=False)
                         .then_inc(s_pe.h, 1))
                else:
                    emit("pe", lambda e, c=c, dd=dd: e.matmul(
                        bank(dd), cp_im3[:, c, dd * 128:(dd + 1) * 128],
                        hsml3[:, c, :], start=False, stop=False))
        pass2_mark = s_pe.inc(1)
        wait("sy", s_pe, pass2_mark)
        waits("sy", [(s_bco, prep_mark["clo"][0]),
                     (s_bco2, prep_mark["clo"][1]), (s_dsy, s_dsy.val)])
        emit("sy", lambda e: e.dma_start(cp_re[:, :], clre)
             .then_inc(s_dsy.h, 16))
        clre_mark = s_dsy.inc(16)
        wait("sy", s_dsy, clre_mark)
        emit("sy", lambda e: e.dma_start(cp_im[:, :], clim)
             .then_inc(s_dsy.h, 16))
        clim_mark = s_dsy.inc(16)
        wait("pe", s_dsy, clre_mark)
        for c in range(DC):
            for dd in range(DC):
                emit("pe", lambda e, c=c, dd=dd: e.matmul(
                    bank(dd), cp_re3[:, c, dd * 128:(dd + 1) * 128],
                    hsrh3[:, c, :], start=False, stop=False))
        wait("pe", s_dsy, clim_mark)
        for c in range(DC):
            for dd in range(DC):
                sp = (c == DC - 1)
                inc_here = sp and dd == DC - 1
                if inc_here:
                    emit("pe", lambda e, c=c, dd=dd: e.matmul(
                        bank(dd), cp_im3[:, c, dd * 128:(dd + 1) * 128],
                        hsmh3[:, c, :], start=False, stop=True)
                         .then_inc(s_pe.h, 1))
                else:
                    emit("pe", lambda e, c=c, dd=dd, sp=sp: e.matmul(
                        bank(dd), cp_im3[:, c, dd * 128:(dd + 1) * 128],
                        hsmh3[:, c, :], start=False, stop=sp))
        att_mark = s_pe.inc(1)

        wait("ve", s_pe, att_mark)
        for dd in range(DC):
            emit("ve", lambda e, dd=dd: e.tensor_tensor(
                x3[:, dd, :], x3[:, dd, :], bank(dd), OP.add))
        vfence()
        for dd in range(DC):
            emit("ve", lambda e, dd=dd: e.scalar_tensor_tensor(
                x3[:, dd, :], xn3[:, dd, :],
                vecs[:, V_DSK + dd:V_DSK + dd + 1], x3[:, dd, :],
                OP.mult, OP.add))
        vfence()
        for dd in range(DC):
            if dd == DC - 1:
                emit("ve", lambda e, dd=dd: e.scalar_tensor_tensor(
                    x3[:, dd, :], xl3[:, dd, :],
                    vecs[:, V_DSK + dd:V_DSK + dd + 1], x3[:, dd, :],
                    OP.mult, OP.add).then_inc(s_ve.h, 1))
            else:
                emit("ve", lambda e, dd=dd: e.scalar_tensor_tensor(
                    x3[:, dd, :], xl3[:, dd, :],
                    vecs[:, V_DSK + dd:V_DSK + dd + 1], x3[:, dd, :],
                    OP.mult, OP.add))
        final_mark = s_ve.inc(1)
        for bk in range(8):
            bank_free[bk] = (s_ve, final_mark)
        last_pe = s_pe.val
        wfree[0] = (s_pe, last_pe)
        wfree[1] = (s_pe, last_pe)
        return (s_ve, final_mark), last_pe

    coef_state["scal"] = emit_coef_scalars()

    x_dep = [(s_dsy, x_mark)]
    x_dep_h0 = None
    for tile in range(NT):
        ln1 = emit_ln(x_dep, produce_lo=False, x_dep_h0=x_dep_h0)
        xm, ffn_pe, _m5h, m4_pe = emit_ffn(ln1, tile)
        x_dep = [xm]
        x_half_marks = _m5h
        if phase_limit >= 2:
            ln2 = emit_ln(x_dep, produce_lo=True)
            am, att_pe = emit_attn(tile, ln2, ffn_pe, m4_pe)
            x_dep = [am]
        if phase_limit >= 3:
            ln3 = emit_ln(x_dep, produce_lo=False)
            xm2, _, _m5h, _ = emit_ffn(ln3, tile)
            x_dep = [xm2]
            x_half_marks = _m5h
        st_marks = []
        for h in range(2):
            waits("sy", [x_half_marks[h], (s_st, s_st.val)])
            emit("sy", lambda e, t=tile, h=h: e.dma_start(
                outT3[:, 4 * h:4 * h + 4, t * T:(t + 1) * T],
                x3[:, 4 * h:4 * h + 4, :]).then_inc(s_st.h, 16))
            st_marks.append(s_st.inc(16))
        if tile + 1 < NT:
            for h in range(2):
                waits("gp", [(s_st, st_marks[h]), (s_ld, s_ld.val)])
                emit("gp", lambda e, t=tile + 1, h=h: e.dma_start(
                    x3[:, 4 * h:4 * h + 4, :],
                    xT3[:, 4 * h:4 * h + 4, t * T:(t + 1) * T])
                     .then_inc(s_ld.h, 16))
                s_ld.inc(16)
            x_dep = [(s_ld, s_ld.val)]
            x_dep_h0 = [(s_ld, s_ld.val - 16)]

    wait("sy", s_dsy, s_dsy.val)
    wait("sy", s_st, s_st.val)
    wait("sy", s_ld, s_ld.val)
    wait("sy", s_wA, s_wA.val)
    wait("sy", s_wB, s_wB.val)
    wait("sy", s_gA, s_gA.val)
    wait("sy", s_gB, s_gB.val)
    wait("sy", s_bco, s_bco.val)
    wait("sy", s_bco2, s_bco2.val)
    wait("sy", s_dgp, s_dgp.val)

    with nc.Block() as block:
        @block.tensor
        def _(e):
            for f in Q["pe"]:
                f(e)

        @block.vector
        def _(e):
            for f in Q["ve"]:
                f(e)

        @block.scalar
        def _(e):
            for f in Q["ac"]:
                f(e)

        @block.gpsimd
        def _(e):
            for f in Q["gp"]:
                f(e)

        @block.sync
        def _(e):
            for f in Q["sy"]:
                f(e)

    ctx.close()
    return nc


_NC_CACHE = {}


def kernel(L0_attn_inputs, L0_attn_nu_log, L0_attn_theta_log, L0_attn_B_re,
           L0_attn_B_im, L0_attn_C_re, L0_attn_C_im, L0_attn_D_skip,
           L0_ffn_w1_w, L0_ffn_w1_s, L0_ffn_w2_w, L0_ffn_w2_s, L0_ffn_h1_w,
           L0_ffn_h1_s, L0_ffn_h2_w, L0_ffn_h2_s, L0_ffn_w3_w, L0_ffn_w3_s):
    f32 = np.float32

    def t(a):
        return np.ascontiguousarray(np.asarray(a, dtype=f32).T)

    def pc(v, c):
        return np.ascontiguousarray(np.asarray(v, dtype=f32).reshape(c, 128).T)

    def wpanel(a, nog, kc):
        a = np.asarray(a, dtype=f32)
        return np.ascontiguousarray(
            a.reshape(nog, 512, kc, 128).transpose(0, 3, 2, 1)
            .reshape(nog, 128, kc * 512))

    def bcpanel(a):
        a = np.asarray(a, dtype=f32)
        return np.ascontiguousarray(
            a.T.reshape(DC, 128, a.shape[0]).transpose(1, 0, 2)
            .reshape(128, DC * a.shape[0]))

    shared = {
        "w1T": wpanel(L0_ffn_w1_w, 8, DC), "w2T": wpanel(L0_ffn_w2_w, 8, DC),
        "h1T": wpanel(L0_ffn_h1_w, 8, HC), "h2T": wpanel(L0_ffn_h2_w, 8, HC),
        "w3T": wpanel(L0_ffn_w3_w, 2, HC),
        "breT": bcpanel(L0_attn_B_re), "bimT": bcpanel(L0_attn_B_im),
        "creT": bcpanel(L0_attn_C_re), "cimT": bcpanel(L0_attn_C_im),
        "s1pc": pc(L0_ffn_w1_s, HC), "s2pc": pc(L0_ffn_w2_s, HC),
        "h1spc": pc(L0_ffn_h1_s, HC), "h2spc": pc(L0_ffn_h2_s, HC),
        "w3spc": pc(L0_ffn_w3_s, DC), "dskpc": pc(L0_attn_D_skip, DC),
        "nulpc": pc(L0_attn_nu_log, DC), "thlpc": pc(L0_attn_theta_log, DC),
    }
    x = np.asarray(L0_attn_inputs, dtype=f32)
    in_maps = []
    for b in range(NCORES):
        m = dict(shared)
        m["xT"] = np.ascontiguousarray(x[b].T)
        in_maps.append(m)

    if "nc" not in _NC_CACHE:
        _NC_CACHE["nc"] = build()
    res = run_bass_kernel_spmd(_NC_CACHE["nc"], in_maps,
                               core_ids=list(range(NCORES)))
    out = np.stack([np.ascontiguousarray(r["outT"].T) for r in res.results])
    return out.astype(np.float32)


# version 16
# speedup vs baseline: 1.1577x; 1.1577x over previous
import numpy as np
import contextlib
import concourse.bass as bass
import concourse.mybir as mybir
from concourse.bass_utils import run_bass_kernel_spmd
from concourse.alu_op_type import AluOpType as OP

F32 = mybir.dt.float32
BF16 = mybir.dt.bfloat16
AF = mybir.ActivationFunctionType

NCORES = 8
D = 1024
DC = D // 128
H = 4096
HC = H // 128
T = 512
NT = 2048 // T
SUB = 128
NSUB = T // SUB
OG = 4
EPS_LN = 1e-5
EPS_REF = 1e-5


class Sem:
    def __init__(self, handle):
        self.h = handle
        self.val = 0

    def inc(self, n=1):
        self.val += n
        return self.val


def build(phase_limit=3):
    nc = bass.Bass("TRN2", target_bir_lowering=False)

    def din(name, shape):
        return nc.dram_tensor(name, list(shape), F32, kind="ExternalInput").ap()

    xT = din("xT", (D, 2048))
    w1T = din("w1T", (8, 128, DC * 512)); w2T = din("w2T", (8, 128, DC * 512))
    h1T = din("h1T", (8, 128, HC * 512)); h2T = din("h2T", (8, 128, HC * 512))
    w3T = din("w3T", (2, 128, HC * 512))
    breT = din("breT", (128, DC * D)); bimT = din("bimT", (128, DC * D))
    creT = din("creT", (128, DC * D)); cimT = din("cimT", (128, DC * D))
    s1pc = din("s1pc", (128, HC)); s2pc = din("s2pc", (128, HC))
    h1spc = din("h1spc", (128, HC)); h2spc = din("h2spc", (128, HC))
    w3spc = din("w3spc", (128, DC)); dskpc = din("dskpc", (128, DC))
    nulpc = din("nulpc", (128, DC)); thlpc = din("thlpc", (128, DC))
    outT = nc.dram_tensor("outT", [D, 2048], F32, kind="ExternalOutput").ap()

    def dscr(name, shape):
        return nc.dram_tensor(name, list(shape), BF16, kind="Internal").ap()

    w1S = dscr("w1S", (8, 128, DC * 512)); w2S = dscr("w2S", (8, 128, DC * 512))
    h1S = dscr("h1S", (8, 128, HC * 512)); h2S = dscr("h2S", (8, 128, HC * 512))
    w3S = dscr("w3S", (2, 128, HC * 512))
    bhre = dscr("bhre", (128, DC * D)); bhim = dscr("bhim", (128, DC * D))
    blre = dscr("blre", (128, DC * D)); blim = dscr("blim", (128, DC * D))
    chre = dscr("chre", (128, DC * D)); chim = dscr("chim", (128, DC * D))
    clre = dscr("clre", (128, DC * D)); clim = dscr("clim", (128, DC * D))

    ctx = contextlib.ExitStack()

    L = {}
    _off = [0]

    def reg(name, nbytes):
        L[name] = _off[0]
        _off[0] += (nbytes + 31) // 32 * 32

    reg("x_t", DC * T * 4)
    reg("xn_t", DC * T * 2)
    reg("xn_lo", DC * T * 2)
    reg("tbig", DC * T * 4)
    reg("A_re", DC * SUB * 4); reg("A_im", DC * SUB * 4)
    reg("Rn_re", DC * SUB * 4); reg("Rn_im", DC * SUB * 4)
    reg("S_re", T * 4); reg("S_im", T * 4)
    reg("ones_t", SUB * 4)
    reg("consts", 64 * 4)
    reg("vecs", 192 * 4)
    reg("coef", 224 * 4)
    reg("prep", 8192)
    reg("u_t", HC * T * 2)
    reg("g_t", HC * T * 2)
    reg("wpanA", HC * T * 2)
    reg("wpanB", HC * T * 2)
    total = _off[0]
    assert total <= 212992, f"SBUF over budget: {total}"

    pool_h = ctx.enter_context(
        nc.sbuf_tensor("pool", [128, total], mybir.dt.uint8))
    pool_base = nc.lookup_mloc(pool_h).addr

    def alias(name, shape, dtype, at, extra=0):
        vt = nc.virtual_sbuf_tensor(name, list(shape), dtype)
        nc._tensor(vt.name, list(shape), dtype, type="SB")
        mloc = nc.lookup_mloc(vt)
        mloc.allocated = True
        mloc.addr = pool_base + L[at] + extra
        return vt

    x_t = alias("x_t", [128, DC * T], F32, "x_t")
    xn_t = alias("xn_t", [128, DC * T], BF16, "xn_t")
    xn_lo = alias("xn_lo", [128, DC * T], BF16, "xn_lo")
    tbig = alias("tbig", [128, DC * T], F32, "tbig")
    A_re = alias("A_re", [128, DC * SUB], F32, "A_re")
    A_im = alias("A_im", [128, DC * SUB], F32, "A_im")
    Rn_re = alias("Rn_re", [128, DC * SUB], F32, "Rn_re")
    Rn_im = alias("Rn_im", [128, DC * SUB], F32, "Rn_im")
    S_re = alias("S_re", [128, T], F32, "S_re")
    S_im = alias("S_im", [128, T], F32, "S_im")
    ones_t = alias("ones_t", [128, SUB], F32, "ones_t")
    consts = alias("consts", [128, 64], F32, "consts")
    vecs = alias("vecs", [128, 192], F32, "vecs")
    coef = alias("coef", [128, 224], F32, "coef")
    prep_f = [alias("prep_f0", [128, 512], F32, "prep"),
              alias("prep_f1", [128, 512], F32, "prep", 4096)]
    prep_hi = [alias("prep_h0", [128, 512], BF16, "prep", 2048),
               alias("prep_h1", [128, 512], BF16, "prep", 6144)]
    prep_lo = [alias("prep_l0", [128, 512], BF16, "prep", 3072),
               alias("prep_l1", [128, 512], BF16, "prep", 7168)]
    u_t = alias("u_t", [128, HC * T], BF16, "u_t")
    g_t = alias("g_t", [128, HC * T], BF16, "g_t")
    wpan = [alias("wpanA", [128, HC * T], BF16, "wpanA"),
            alias("wpanB", [128, HC * T], BF16, "wpanB")]
    bh_re = alias("bh_re", [128, DC * D], BF16, "u_t")
    bh_im = alias("bh_im", [128, DC * D], BF16, "u_t", DC * D * 2)
    bl_re = alias("bl_re", [128, DC * D], BF16, "g_t")
    bl_im = alias("bl_im", [128, DC * D], BF16, "g_t", DC * D * 2)
    cp_re = alias("cp_re", [128, DC * D], BF16, "wpanA")
    cp_im = alias("cp_im", [128, DC * D], BF16, "wpanA", DC * D * 2)
    hsrh = alias("hsrh", [128, DC * T], BF16, "wpanB")
    hsrl = alias("hsrl", [128, DC * T], BF16, "wpanB", DC * T * 2)
    hsmh = alias("hsmh", [128, DC * T], BF16, "wpanB", DC * T * 4)
    hsml = alias("hsml", [128, DC * T], BF16, "wpanB", DC * T * 6)

    K_ONECOL, K_PIHALF, K_EPSLN, K_FENCE = 0, 1, 2, 3
    V_S1, V_S2, V_H1S, V_H2S = 0, 32, 64, 96
    V_W3S, V_HALFW3S, V_DSK, V_NUL, V_THL = 128, 136, 144, 152, 160
    C_NU, C_TH, C_EMN, C_EPN = 0, 8, 16, 24
    C_SIN, C_COS, C_GAM = 32, 40, 48
    C_LRE, C_LIM, C_QRE, C_QIM = 56, 64, 72, 80
    C_CT1, C_CT2, C_CT3, C_CT4 = 88, 96, 104, 112
    C_IRE, C_IIM, C_WRE, C_WIM = 120, 128, 136, 144
    C_Q1, C_Q2, C_Q3, C_Q4 = 152, 153, 154, 155
    R_NEGMU, R_M2, R_TMP, R_VAR, R_RSTD, R_NMR = 0, 512, 1024, 1536, 2048, 2560
    LADDER = 3072
    SQ0, SQ1 = 3072, 3584
    LQRE, LQIM, LPRE, LPIM = LADDER, LADDER + 64, LADDER + 128, LADDER + 192

    psum = ctx.enter_context(nc.psum_tensor("ps", [128, 4096], F32))

    def bank(i, w=T):
        return psum[:, i * 512:i * 512 + w]

    def bank_row(i, w=T):
        return psum[0:1, i * 512:i * 512 + w]

    def bank_sub(i, s):
        return psum[:, i * 512 + s * SUB:i * 512 + (s + 1) * SUB]

    s_pe = Sem(ctx.enter_context(nc.semaphore("s_pe")))
    s_ve = Sem(ctx.enter_context(nc.semaphore("s_ve")))
    s_ac = Sem(ctx.enter_context(nc.semaphore("s_ac")))
    s_dsy = Sem(ctx.enter_context(nc.semaphore("s_dsy")))
    s_dgp = Sem(ctx.enter_context(nc.semaphore("s_dgp")))
    s_po = Sem(ctx.enter_context(nc.semaphore("s_po")))
    s_bco = Sem(ctx.enter_context(nc.semaphore("s_bco")))
    s_bco2 = Sem(ctx.enter_context(nc.semaphore("s_bco2")))
    s_wA = Sem(ctx.enter_context(nc.semaphore("s_wA")))
    s_wB = Sem(ctx.enter_context(nc.semaphore("s_wB")))
    s_gA = Sem(ctx.enter_context(nc.semaphore("s_gA")))
    s_gB = Sem(ctx.enter_context(nc.semaphore("s_gB")))
    s_st = Sem(ctx.enter_context(nc.semaphore("s_st")))
    s_ld = Sem(ctx.enter_context(nc.semaphore("s_ld")))

    Q = {"pe": [], "ve": [], "ac": [], "gp": [], "sy": []}

    def emit(q, fn):
        Q[q].append(fn)

    def wait(q, sem, val):
        if sem is not None and val > 0:
            emit(q, lambda e, s=sem.h, v=val: e.wait_ge(s, v))

    def waits(q, pairs):
        best = {}
        for p in pairs:
            if p is None:
                continue
            sem, val = p
            if sem is None or val <= 0:
                continue
            k = id(sem.h)
            if k not in best or best[k][1] < val:
                best[k] = (sem, val)
        for sem, val in best.values():
            wait(q, sem, val)

    def vfence():
        emit("ve", lambda e: e.memset(consts[:, K_FENCE:K_FENCE + 1], 0.0)
             .then_inc(s_ve.h, 1))
        wait("ve", s_ve, s_ve.inc(1))

    bank_free = [None] * 8
    wfree = [None, None]
    parity = [0]

    def claim_banks(banks, extra=()):
        waits("pe", [bank_free[i] for i in banks] + list(extra))

    prep_mark = {}

    def emit_casts(stages):
        for name, dst, srcw, nog in stages:
            for og in range(nog):
                emit("gp", lambda e, d=dst[og], s=srcw[og]:
                     e.dma_start(d, s).then_inc(s_dgp.h, 16))
                wait("gp", s_dgp, s_dgp.inc(16))
                prep_mark[(name, og)] = s_dgp.val

    emit_casts((("w3S", w3S, w3T, 2),))
    for name, dst, srcw in (("bhre", bhre, breT), ("bhim", bhim, bimT),
                            ("chre", chre, creT), ("chim", chim, cimT)):
        emit("gp", lambda e, d=dst, s=srcw:
             e.dma_start(d, s).then_inc(s_dgp.h, 16))
        wait("gp", s_dgp, s_dgp.inc(16))
        prep_mark[name] = s_dgp.val

    def ldvec(dst_off, width, src):
        emit("sy", lambda e, d=vecs[:, dst_off:dst_off + width], s=src:
             e.dma_start(d, s).then_inc(s_dsy.h, 16))
        return s_dsy.inc(16)

    ldvec(V_S1, HC, s1pc); ldvec(V_S2, HC, s2pc)
    ldvec(V_H1S, HC, h1spc); ldvec(V_H2S, HC, h2spc)
    ldvec(V_W3S, DC, w3spc); ldvec(V_DSK, DC, dskpc)
    ldvec(V_NUL, DC, nulpc)
    vec_mark = ldvec(V_THL, DC, thlpc)

    xT3 = xT.rearrange("(c p) t -> p c t", p=128)
    outT3 = outT.rearrange("(c p) t -> p c t", p=128)
    x3 = x_t.ap().rearrange("p (c t) -> p c t", t=T)
    xn3 = xn_t.ap().rearrange("p (c t) -> p c t", t=T)
    xl3 = xn_lo.ap().rearrange("p (c t) -> p c t", t=T)
    tb3 = tbig.ap().rearrange("p (c t) -> p c t", t=T)

    def emit_x_load(tile, dep):
        waits("sy", [dep])
        wait("sy", s_dsy, s_dsy.val)
        emit("sy", lambda e, t=tile: e.dma_start(
            x3, xT3[:, :, t * T:(t + 1) * T]).then_inc(s_dsy.h, 16))
        return s_dsy.inc(16)

    x_mark = emit_x_load(0, None)

    emit("ve", lambda e: e.memset(ones_t[:, :], 1.0))
    emit("ve", lambda e: e.memset(consts[:, K_ONECOL:K_ONECOL + 1], 1.0))
    emit("ve", lambda e: e.memset(consts[:, K_PIHALF:K_PIHALF + 1],
                                  float(np.pi / 2)))
    emit("ve", lambda e: e.memset(consts[:, K_EPSLN:K_EPSLN + 1], float(EPS_LN)))
    emit("ve", lambda e: e.memset(coef[:, C_IRE:C_IRE + DC], 0.0))
    emit("ve", lambda e: e.memset(coef[:, C_IIM:C_IIM + DC], 0.0)
         .then_inc(s_ve.h, 1))
    consts_mark = s_ve.inc(1)

    wait("ve", s_dsy, vec_mark)
    emit("ve", lambda e: e.tensor_scalar(
        vecs[:, V_HALFW3S:V_HALFW3S + DC], vecs[:, V_W3S:V_W3S + DC],
        0.5, None, OP.mult).then_inc(s_ve.h, 1))
    s_ve.inc(1)

    def cc(i, w=DC):
        return coef[:, i:i + w]

    def emit_coef_scalars():
        waits("ac", [(s_dsy, vec_mark), (s_ve, consts_mark)])
        emit("ac", lambda e: e.activation(cc(C_NU), vecs[:, V_NUL:V_NUL + DC],
                                          AF.Exp))
        emit("ac", lambda e: e.activation(cc(C_TH), vecs[:, V_THL:V_THL + DC],
                                          AF.Exp).then_inc(s_ac.h, 1))
        wait("ac", s_ac, s_ac.inc(1))
        emit("ac", lambda e: e.activation(cc(C_EMN), cc(C_NU), AF.Exp,
                                          scale=-1.0))
        emit("ac", lambda e: e.activation(cc(C_EPN), cc(C_NU), AF.Exp))
        emit("ac", lambda e: e.activation(cc(C_SIN), cc(C_TH), AF.Sin))
        emit("ac", lambda e: e.activation(cc(C_COS), cc(C_TH), AF.Sin,
                                          scale=-1.0,
                                          bias=consts[:, K_PIHALF:K_PIHALF + 1]))
        emit("ac", lambda e: e.activation(cc(C_CT1), cc(C_NU), AF.Exp,
                                          scale=-2.0).then_inc(s_ac.h, 1))
        m1 = s_ac.inc(1)
        wait("ve", s_ac, m1)
        emit("ve", lambda e: e.tensor_scalar(cc(C_CT2), cc(C_CT1), -1.0,
                                             1.0 + EPS_REF, OP.mult, OP.add)
             .then_inc(s_ve.h, 1))
        m2 = s_ve.inc(1)
        wait("ac", s_ve, m2)
        emit("ac", lambda e: e.activation(cc(C_CT3), cc(C_CT2), AF.Ln)
             .then_inc(s_ac.h, 1))
        wait("ac", s_ac, s_ac.inc(1))
        emit("ac", lambda e: e.activation(cc(C_GAM), cc(C_CT3), AF.Exp,
                                          scale=0.5).then_inc(s_ac.h, 1))
        return s_ac.inc(1)

    def emit_bc_prep():
        chunks = []
        for srcT, hiS, loS in ((breT, bhre, blre), (bimT, bhim, blim),
                               (creT, chre, clre), (cimT, chim, clim)):
            for j in range(16):
                cs = slice(j * 512, (j + 1) * 512)
                chunks.append((srcT, hiS, loS, cs))
        in_marks = [None] * len(chunks)

        def issue_in(i):
            srcT, hiS, loS, cs = chunks[i]
            buf = i % 2
            emit("gp", lambda e, s=srcT[:, cs], b=buf:
                 e.dma_start(prep_f[b][:, :], s).then_inc(s_dgp.h, 16))
            s_dgp.inc(16)
            emit("gp", lambda e, s=hiS[:, cs], b=buf:
                 e.dma_start(prep_hi[b][:, :], s).then_inc(s_dgp.h, 16))
            in_marks[i] = s_dgp.inc(16)

        issue_in(0)
        out_marks = [0] * len(chunks)
        for i in range(len(chunks)):
            srcT, hiS, loS, cs = chunks[i]
            buf = i % 2
            wait("gp", s_dgp, in_marks[i])
            if i + 1 < len(chunks):
                issue_in(i + 1)
            osem = s_bco if buf == 0 else s_bco2
            if i >= 2:
                wait("gp", osem, out_marks[i - 2])
            emit("gp", lambda e, b=buf: e.tensor_tensor(
                prep_lo[b][:, :], prep_f[b][:, :], prep_hi[b][:, :],
                OP.subtract).then_inc(s_po.h, 1))
            wait("gp", s_po, s_po.inc(1))
            emit("gp", lambda e, d=loS[:, cs], b=buf, sm=osem.h:
                 e.dma_start(d, prep_lo[b][:, :]).then_inc(sm, 16))
            out_marks[i] = osem.inc(16)
            if i == 31:
                prep_mark["blo"] = (s_bco.val, s_bco2.val)
        prep_mark["clo"] = (s_bco.val, s_bco2.val)

    def emit_coef_arrays(scal_mark):
        wait("ve", s_ac, scal_mark)
        rows = tbig
        emit("ve", lambda e: e.tensor_tensor(cc(C_LRE), cc(C_EMN), cc(C_COS),
                                             OP.mult))
        emit("ve", lambda e: e.tensor_tensor(cc(C_LIM), cc(C_EMN), cc(C_SIN),
                                             OP.mult))
        emit("ve", lambda e: e.tensor_tensor(cc(C_QRE), cc(C_EPN), cc(C_COS),
                                             OP.mult))
        emit("ve", lambda e: e.scalar_tensor_tensor(cc(C_QIM), cc(C_EPN), -1.0,
                                                    cc(C_SIN), OP.mult,
                                                    OP.mult))
        vfence()
        emit("ve", lambda e: e.tensor_copy(rows[:, LQRE:LQRE + DC], cc(C_QRE)))
        emit("ve", lambda e: e.tensor_copy(rows[:, LQIM:LQIM + DC], cc(C_QIM)))
        emit("ve", lambda e: e.tensor_copy(rows[:, LPRE:LPRE + DC], cc(C_LRE)))
        emit("ve", lambda e: e.tensor_copy(rows[:, LPIM:LPIM + DC], cc(C_LIM)))
        vfence()
        for k in range(7):
            qre = rows[:, LQRE + k * DC:LQRE + (k + 1) * DC]
            qim = rows[:, LQIM + k * DC:LQIM + (k + 1) * DC]
            pre = rows[:, LPRE + k * DC:LPRE + (k + 1) * DC]
            pim = rows[:, LPIM + k * DC:LPIM + (k + 1) * DC]
            emit("ve", lambda e, a=qre: e.tensor_tensor(cc(C_CT1), a, a,
                                                        OP.mult))
            emit("ve", lambda e, a=qim: e.tensor_tensor(cc(C_CT2), a, a,
                                                        OP.mult))
            emit("ve", lambda e, a=pre: e.tensor_tensor(cc(C_CT3), a, a,
                                                        OP.mult))
            emit("ve", lambda e, a=pim: e.tensor_tensor(cc(C_CT4), a, a,
                                                        OP.mult))
            vfence()
            emit("ve", lambda e, a=qre, b=qim,
                 d=rows[:, LQIM + (k + 1) * DC:LQIM + (k + 2) * DC]:
                 e.scalar_tensor_tensor(d, a, 2.0, b, OP.mult, OP.mult))
            emit("ve", lambda e, a=pre, b=pim,
                 d=rows[:, LPIM + (k + 1) * DC:LPIM + (k + 2) * DC]:
                 e.scalar_tensor_tensor(d, a, 2.0, b, OP.mult, OP.mult))
            emit("ve", lambda e,
                 d=rows[:, LQRE + (k + 1) * DC:LQRE + (k + 2) * DC]:
                 e.tensor_tensor(d, cc(C_CT1), cc(C_CT2), OP.subtract))
            emit("ve", lambda e,
                 d=rows[:, LPRE + (k + 1) * DC:LPRE + (k + 2) * DC]:
                 e.tensor_tensor(d, cc(C_CT3), cc(C_CT4), OP.subtract))
            vfence()
        emit("ve", lambda e: e.tensor_copy(
            cc(C_WRE), rows[:, LPRE + 7 * DC:LPRE + 8 * DC]))
        emit("ve", lambda e: e.tensor_copy(
            cc(C_WIM), rows[:, LPIM + 7 * DC:LPIM + 8 * DC]))
        vfence()

        st_re = [tbig[:, c * SUB:(c + 1) * SUB] for c in range(DC)]
        st_im = [tbig[:, 1024 + c * SUB:1024 + (c + 1) * SUB]
                 for c in range(DC)]
        tmpu = [tbig[:, 2048 + i * SUB:2048 + (i + 1) * SUB] for i in range(4)]
        tmpv = [tbig[:, 2560 + i * SUB:2560 + (i + 1) * SUB] for i in range(4)]

        def build_arr(dst_re, dst_im, lre, lim, base_kind):
            for c in range(DC):
                if base_kind == "gamma":
                    emit("ve", lambda e, c=c: e.tensor_copy(
                        st_re[c][:, 0:1], coef[:, C_GAM + c:C_GAM + c + 1]))
                else:
                    emit("ve", lambda e, c=c: e.memset(st_re[c][:, 0:1], -1.0))
                emit("ve", lambda e, c=c: e.memset(st_im[c][:, 0:1], 0.0))
            vfence()
            n = 1
            for k in range(7):
                for g0 in range(0, DC, 4):
                    for i, c in enumerate(range(g0, g0 + 4)):
                        pr = rows[:, lre + k * DC + c:lre + k * DC + c + 1]
                        pi_ = rows[:, lim + k * DC + c:lim + k * DC + c + 1]
                        emit("ve", lambda e, c=c, pi_=pi_, u=tmpu[i], n=n:
                             e.tensor_scalar(u[:, 0:n], st_im[c][:, 0:n],
                                             pi_, None, OP.mult))
                        emit("ve", lambda e, c=c, pr=pr, v=tmpv[i], n=n:
                             e.tensor_scalar(v[:, 0:n], st_im[c][:, 0:n],
                                             pr, None, OP.mult))
                    vfence()
                    for i, c in enumerate(range(g0, g0 + 4)):
                        pr = rows[:, lre + k * DC + c:lre + k * DC + c + 1]
                        pi_ = rows[:, lim + k * DC + c:lim + k * DC + c + 1]
                        emit("ve", lambda e, c=c, pr=pr, u=tmpu[i], n=n:
                             e.scalar_tensor_tensor(
                                 st_re[c][:, n:2 * n], st_re[c][:, 0:n], pr,
                                 u[:, 0:n], OP.mult, OP.subtract))
                        emit("ve", lambda e, c=c, pi_=pi_, v=tmpv[i], n=n:
                             e.scalar_tensor_tensor(
                                 st_im[c][:, n:2 * n], st_re[c][:, 0:n], pi_,
                                 v[:, 0:n], OP.mult, OP.add))
                    vfence()
                n *= 2
            emit("ve", lambda e, d=dst_re: e.tensor_copy(
                d[:, :], tbig[:, 0:DC * SUB]))
            emit("ve", lambda e, d=dst_im: e.tensor_copy(
                d[:, :], tbig[:, 1024:1024 + DC * SUB]))
            vfence()

        build_arr(A_re, A_im, LQRE, LQIM, "gamma")
        build_arr(Rn_re, Rn_im, LPRE, LPIM, "neg1")
        emit("ve", lambda e: e.memset(consts[:, K_FENCE:K_FENCE + 1], 0.0)
             .then_inc(s_ve.h, 1))
        return s_ve.inc(1)

    def row(o, w=T):
        return tbig[0:1, o:o + w]

    def emit_ln(x_dep_pairs, produce_lo, x_dep_h0=None):
        h0 = x_dep_h0 if x_dep_h0 is not None else x_dep_pairs
        claim_banks([0, 1], extra=list(h0) + [(s_ve, consts_mark)])
        for dk in range(DC):
            if dk == 4:
                waits("pe", x_dep_pairs)
            emit("pe", lambda e, dk=dk: e.matmul(
                bank_row(0), consts[:, K_ONECOL:K_ONECOL + 1], x3[:, dk, :],
                start=(dk == 0), stop=(dk == DC - 1)))
        waits("ac", h0)
        sq = [tbig[:, SQ0:SQ0 + T], tbig[:, SQ1:SQ1 + T]]
        sq_ac, pe_sq = [], []
        for dk in range(DC):
            dst = sq[dk % 2]
            if dk == 4:
                waits("ac", x_dep_pairs)
            if dk >= 2:
                wait("ac", s_pe, pe_sq[dk - 2])
            emit("ac", lambda e, dst=dst, dk=dk: e.activation(
                dst, x3[:, dk, :], AF.Square).then_inc(s_ac.h, 1))
            sq_ac.append(s_ac.inc(1))
            wait("pe", s_ac, sq_ac[dk])
            emit("pe", lambda e, src=dst, dk=dk: e.matmul(
                bank_row(1), consts[:, K_ONECOL:K_ONECOL + 1], src,
                start=(dk == 0), stop=(dk == DC - 1)).then_inc(s_pe.h, 1))
            pe_sq.append(s_pe.inc(1))
        stats_mark = pe_sq[-1]

        wait("ve", s_pe, stats_mark)
        emit("ve", lambda e: e.tensor_scalar(row(R_NEGMU), bank_row(0),
                                             -1.0 / D, None, OP.mult))
        emit("ve", lambda e: e.tensor_scalar(row(R_M2), bank_row(1),
                                             1.0 / D, None, OP.mult)
             .then_inc(s_ve.h, 1))
        mu_mark = s_ve.inc(1)
        wait("ac", s_ve, mu_mark)
        emit("ac", lambda e: e.activation(row(R_TMP), row(R_NEGMU), AF.Square)
             .then_inc(s_ac.h, 1))
        sqm_mark = s_ac.inc(1)
        wait("ve", s_ac, sqm_mark)
        emit("ve", lambda e: e.tensor_tensor(row(R_VAR), row(R_M2), row(R_TMP),
                                             OP.subtract).then_inc(s_ve.h, 1))
        var_mark = s_ve.inc(1)
        wait("ac", s_ve, var_mark)
        emit("ac", lambda e: e.activation(row(R_RSTD), row(R_VAR), AF.Ln,
                                          bias=consts[0:1, K_EPSLN:K_EPSLN + 1])
             .then_inc(s_ac.h, 1))
        wait("ac", s_ac, s_ac.inc(1))
        emit("ac", lambda e: e.activation(row(R_RSTD), row(R_RSTD), AF.Exp,
                                          scale=-0.5).then_inc(s_ac.h, 1))
        rstd_mark = s_ac.inc(1)
        wait("ve", s_ac, rstd_mark)
        emit("ve", lambda e: e.tensor_tensor(row(R_NMR), row(R_NEGMU),
                                             row(R_RSTD), OP.mult)
             .then_inc(s_ve.h, 1))
        nmr_mark = s_ve.inc(1)
        claim_banks([2, 3], extra=[(s_ve, nmr_mark)])
        emit("pe", lambda e: e.matmul(bank(2), ones_t[0:1, 0:128], row(R_RSTD),
                                      start=True, stop=True))
        emit("pe", lambda e: e.matmul(bank(3), ones_t[0:1, 0:128], row(R_NMR),
                                      start=True, stop=True)
             .then_inc(s_pe.h, 1))
        bc_mark = s_pe.inc(1)
        wait("ve", s_pe, bc_mark)
        for dk in range(DC):
            emit("ve", lambda e, dk=dk: e.tensor_tensor(
                tb3[:, dk, :], x3[:, dk, :], bank(2), OP.mult))
        vfence()
        for dk in range(DC):
            last = (dk == DC - 1) and not produce_lo
            if last:
                emit("ve", lambda e, dk=dk: e.tensor_tensor(
                    xn3[:, dk, :], tb3[:, dk, :], bank(3), OP.add)
                     .then_inc(s_ve.h, 1))
            else:
                emit("ve", lambda e, dk=dk: e.tensor_tensor(
                    xn3[:, dk, :], tb3[:, dk, :], bank(3), OP.add))
        if not produce_lo:
            apply_mark = s_ve.inc(1)
        else:
            vfence()
            for dk in range(DC):
                emit("ve", lambda e, dk=dk: e.tensor_tensor(
                    tb3[:, dk, :], tb3[:, dk, :], xn3[:, dk, :], OP.subtract))
            vfence()
            for dk in range(DC):
                if dk == DC - 1:
                    emit("ve", lambda e, dk=dk: e.tensor_tensor(
                        xl3[:, dk, :], tb3[:, dk, :], bank(3), OP.add)
                         .then_inc(s_ve.h, 1))
                else:
                    emit("ve", lambda e, dk=dk: e.tensor_tensor(
                        xl3[:, dk, :], tb3[:, dk, :], bank(3), OP.add))
            apply_mark = s_ve.inc(1)
        bank_free[0] = (s_ve, mu_mark)
        bank_free[1] = (s_ve, mu_mark)
        bank_free[2] = (s_ve, apply_mark)
        bank_free[3] = (s_ve, apply_mark)
        return apply_mark

    def og_load_cast(srcF, KC, og, par):
        sem = s_gA if par == 0 else s_gB
        waits("gp", [wfree[par], (sem, sem.val)])
        emit("gp", lambda e, d=wpan[par][:, 0:KC * 512], s=srcF[og], sm=sem.h:
             e.dma_start(d, s).then_inc(sm, 16))
        sem.inc(16)
        return (sem, sem.val)

    def og_load(scrS, KC, og, par, prepname):
        sem = s_wA if par == 0 else s_wB
        waits("sy", [wfree[par], (s_dgp, prep_mark[(prepname, og)]),
                     (sem, sem.val)])
        emit("sy", lambda e, d=wpan[par][:, 0:KC * 512], s=scrS[og], sm=sem.h:
             e.dma_start(d, s).then_inc(sm, 16))
        sem.inc(16)
        return (sem, sem.val)

    def ffn_stage(scrS, prepname, KC, OC, rhs_t, evac, rhs_dep,
                  cast_src=None):
        rhs3 = rhs_t.ap().rearrange("p (k t) -> p k t", t=T)
        marks = []
        for og in range(OC // OG):
            par = parity[0]
            parity[0] ^= 1
            if cast_src is not None:
                dmark = og_load_cast(cast_src, KC, og, par)
            else:
                dmark = og_load(scrS, KC, og, par, prepname)
            banks = [par * 4 + i for i in range(OG)]
            claim_banks(banks, extra=[dmark] + list(rhs_dep))
            for k in range(KC):
                for oc in range(OG):
                    st, sp = (k == 0), (k == KC - 1)
                    w_ap = wpan[par][:, k * 512 + oc * 128:
                                     k * 512 + oc * 128 + 128]
                    if sp and oc == OG - 1:
                        emit("pe", lambda e, w=w_ap, r=rhs3[:, k, :],
                             o=bank(banks[oc]), st=st:
                             e.matmul(o, w, r, start=st, stop=True)
                             .then_inc(s_pe.h, 1))
                    else:
                        emit("pe", lambda e, w=w_ap, r=rhs3[:, k, :],
                             o=bank(banks[oc]), st=st, sp=sp:
                             e.matmul(o, w, r, start=st, stop=sp))
            grp = s_pe.inc(1)
            wfree[par] = (s_pe, grp)
            ev_mark = evac(og, banks, grp)
            for bk in banks:
                bank_free[bk] = ev_mark
            marks.append((grp, ev_mark))
        return marks

    def evac_act(dst_t, func, svec_off):
        dst3 = dst_t.ap().rearrange("p (k t) -> p k t", t=T)

        def f(og, banks, grp):
            wait("ac", s_pe, grp)
            for i, bk in enumerate(banks):
                ocg = og * OG + i
                sc = vecs[:, svec_off + ocg:svec_off + ocg + 1]
                if i == OG - 1:
                    emit("ac", lambda e, bk=bk, ocg=ocg, sc=sc: e.activation(
                        dst3[:, ocg, :], bank(bk), func, scale=sc)
                         .then_inc(s_ac.h, 1))
                else:
                    emit("ac", lambda e, bk=bk, ocg=ocg, sc=sc: e.activation(
                        dst3[:, ocg, :], bank(bk), func, scale=sc))
            return (s_ac, s_ac.inc(1))
        return f

    def evac_mul(dst_t, a_t, svec_off):
        dst3 = dst_t.ap().rearrange("p (k t) -> p k t", t=T)
        a3 = a_t.ap().rearrange("p (k t) -> p k t", t=T)

        def f(og, banks, grp):
            wait("ve", s_pe, grp)
            for i, bk in enumerate(banks):
                ocg = og * OG + i
                sc = vecs[:, svec_off + ocg:svec_off + ocg + 1]
                if i == OG - 1:
                    emit("ve", lambda e, bk=bk, ocg=ocg, sc=sc:
                         e.scalar_tensor_tensor(dst3[:, ocg, :], bank(bk), sc,
                                                a3[:, ocg, :], OP.mult,
                                                OP.mult).then_inc(s_ve.h, 1))
                else:
                    emit("ve", lambda e, bk=bk, ocg=ocg, sc=sc:
                         e.scalar_tensor_tensor(dst3[:, ocg, :], bank(bk), sc,
                                                a3[:, ocg, :], OP.mult,
                                                OP.mult))
            return (s_ve, s_ve.inc(1))
        return f

    def evac_resid(svec_off):
        def f(og, banks, grp):
            wait("ve", s_pe, grp)
            for i, bk in enumerate(banks):
                ocg = og * OG + i
                sc = vecs[:, svec_off + ocg:svec_off + ocg + 1]
                if i == OG - 1:
                    emit("ve", lambda e, bk=bk, ocg=ocg, sc=sc:
                         e.scalar_tensor_tensor(x3[:, ocg, :], bank(bk), sc,
                                                x3[:, ocg, :], OP.mult,
                                                OP.add).then_inc(s_ve.h, 1))
                else:
                    emit("ve", lambda e, bk=bk, ocg=ocg, sc=sc:
                         e.scalar_tensor_tensor(x3[:, ocg, :], bank(bk), sc,
                                                x3[:, ocg, :], OP.mult,
                                                OP.add))
            return (s_ve, s_ve.inc(1))
        return f

    coef_state = {"pending": True, "scal": None, "mark": None, "hcast": []}

    def emit_ffn(xn_mark, tile):
        dep = [(s_ve, xn_mark)]
        first0 = tile == 0 and coef_state["pending"]
        m1 = ffn_stage(w1S, "w1S", DC, HC, xn_t,
                       evac_act(g_t, AF.Silu, V_S1), dep,
                       cast_src=w1T if first0 else None)
        a_done = m1[-1][1]
        m2 = ffn_stage(w2S, "w2S", DC, HC, xn_t,
                       evac_mul(u_t, g_t, V_S2), dep + [a_done],
                       cast_src=w2T if first0 else None)
        u_done = m2[-1][1]
        first = tile == 0 and coef_state["pending"]
        if first:
            coef_state["mark"] = emit_coef_arrays(coef_state["scal"])
            coef_state["pending"] = False
        m3 = ffn_stage(h1S, "h1S", HC, HC, u_t,
                       evac_act(g_t, AF.Gelu, V_H1S), [u_done],
                       cast_src=h1T if first else None)
        g_done = m3[-1][1]
        m4 = ffn_stage(h2S, "h2S", HC, HC, g_t,
                       evac_act(u_t, AF.Gelu, V_H2S), [g_done],
                       cast_src=h2T if first else None)
        g2_done = m4[-1][1]
        m5 = ffn_stage(w3S, "w3S", HC, DC, u_t,
                       evac_resid(V_HALFW3S), [g2_done])
        if first:
            emit_bc_prep()
            coef_state["hcast"] = (
                [("h1S", h1S, h1T, og) for og in range(8)]
                + [("h2S", h2S, h2T, og) for og in range(8)]
                + [("w1S", w1S, w1T, og) for og in range(8)]
                + [("w2S", w2S, w2T, og) for og in range(8)])
        return m5[-1][1], s_pe.val, [m5[0][1], m5[1][1]], m4[-1][0]

    bh_re3 = bh_re.ap().rearrange("p (k h) -> p k h", h=D)
    bh_im3 = bh_im.ap().rearrange("p (k h) -> p k h", h=D)
    bl_re3 = bl_re.ap().rearrange("p (k h) -> p k h", h=D)
    bl_im3 = bl_im.ap().rearrange("p (k h) -> p k h", h=D)
    cp_re3 = cp_re.ap().rearrange("p (k d) -> p k d", d=D)
    cp_im3 = cp_im.ap().rearrange("p (k d) -> p k d", d=D)
    hsrh3 = hsrh.ap().rearrange("p (c t) -> p c t", t=T)
    hsrl3 = hsrl.ap().rearrange("p (c t) -> p c t", t=T)
    hsmh3 = hsmh.ap().rearrange("p (c t) -> p c t", t=T)
    hsml3 = hsml.ap().rearrange("p (c t) -> p c t", t=T)
    A_re3 = A_re.ap().rearrange("p (c j) -> p c j", j=SUB)
    A_im3 = A_im.ap().rearrange("p (c j) -> p c j", j=SUB)
    Rn_re3 = Rn_re.ap().rearrange("p (c j) -> p c j", j=SUB)
    Rn_im3 = Rn_im.ap().rearrange("p (c j) -> p c j", j=SUB)
    def emit_attn(tile, xn_mark, ffn1_pe_mark, m4_pe_mark):
        waits("sy", [(s_pe, m4_pe_mark), (s_bco, prep_mark["blo"][0]),
                     (s_bco2, prep_mark["blo"][1]), (s_dsy, s_dsy.val)])
        for dt_, s3 in ((bl_re, blre), (bl_im, blim)):
            emit("sy", lambda e, d=dt_, s=s3:
                 e.dma_start(d[:, :], s).then_inc(s_dsy.h, 16))
            s_dsy.inc(16)
        waits("sy", [(s_pe, ffn1_pe_mark), (s_dgp, prep_mark["bhim"]),
                     (s_dsy, s_dsy.val)])
        for dt_, s3 in ((bh_re, bhre), (bh_im, bhim)):
            emit("sy", lambda e, d=dt_, s=s3:
                 e.dma_start(d[:, :], s).then_inc(s_dsy.h, 16))
            s_dsy.inc(16)
        bpan_mark = s_dsy.val
        waits("sy", [wfree[0], wfree[1], (s_dgp, prep_mark["chim"]),
                     (s_dsy, s_dsy.val)])
        for dt_, s3 in ((cp_re, chre), (cp_im, chim)):
            emit("sy", lambda e, d=dt_, s=s3:
                 e.dma_start(d[:, :], s).then_inc(s_dsy.h, 16))
            s_dsy.inc(16)
        chi_mark = s_dsy.val

        hs_marks = []
        for c in range(DC):
            pair = (c % 2) * 2
            extra = [(s_dsy, bpan_mark), (s_ve, xn_mark)]
            if tile == 0 and c == 0 and coef_state["mark"] is not None:
                extra.append((s_ve, coef_state["mark"]))
            claim_banks([pair, pair + 1], extra=extra)
            passes = [(bh_re3, bh_im3, xn3), (bl_re3, bl_im3, xn3),
                      (bh_re3, bh_im3, xl3)]
            for p_idx, (pre3, pim3, rhs3) in enumerate(passes):
                first = (p_idx == 0)
                last = (p_idx == len(passes) - 1)
                for dk in range(DC):
                    emit("pe", lambda e, dk=dk, c=c, p3=pre3, r3=rhs3, pb=pair,
                         st=(first and dk == 0), sp=(last and dk == DC - 1):
                         e.matmul(bank(pb), p3[:, dk, c * 128:(c + 1) * 128],
                                  r3[:, dk, :], start=st, stop=sp))
                for dk in range(DC):
                    st = first and dk == 0
                    sp = last and dk == DC - 1
                    if sp:
                        emit("pe", lambda e, dk=dk, c=c, p3=pim3, r3=rhs3,
                             pb=pair, st=st:
                             e.matmul(bank(pb + 1),
                                      p3[:, dk, c * 128:(c + 1) * 128],
                                      r3[:, dk, :], start=st, stop=True)
                             .then_inc(s_pe.h, 1))
                    else:
                        emit("pe", lambda e, dk=dk, c=c, p3=pim3, r3=rhs3,
                             pb=pair, st=st:
                             e.matmul(bank(pb + 1),
                                      p3[:, dk, c * 128:(c + 1) * 128],
                                      r3[:, dk, :], start=st, stop=False))
            bu_mark = s_pe.inc(1)

            if coef_state.get("hcast"):
                wait("gp", s_pe, bu_mark)
                for _ in range(4):
                    nm, dstS, srcF, og = coef_state["hcast"].pop(0)
                    emit("gp", lambda e, d=dstS[og], s=srcF[og]:
                         e.dma_start(d, s).then_inc(s_dgp.h, 16))
                    wait("gp", s_dgp, s_dgp.inc(16))
                    prep_mark[(nm, og)] = s_dgp.val

            wait("ve", s_pe, bu_mark)
            vfence()
            for s in range(NSUB):
                bre = bank_sub(pair, s)
                bim = bank_sub(pair + 1, s)
                for i, (arr3, b) in enumerate(((A_re3, bre), (A_im3, bim),
                                               (A_im3, bre), (A_re3, bim))):
                    q = (s * 4 + i) * SUB
                    emit("ve", lambda e, c=c, a3=arr3, b=b, q=q:
                         e.tensor_tensor(tbig[:, q:q + SUB], a3[:, c, :], b,
                                         OP.mult))
            vfence()
            bankrel = s_ve.val
            for s in range(NSUB):
                q = s * 4 * SUB
                emit("ve", lambda e, q=q, d=2048 + s * SUB:
                     e.tensor_tensor(tbig[:, d:d + SUB], tbig[:, q:q + SUB],
                                     tbig[:, q + SUB:q + 2 * SUB],
                                     OP.subtract))
                emit("ve", lambda e, q=q, d=2560 + s * SUB:
                     e.tensor_tensor(tbig[:, d:d + SUB],
                                     tbig[:, q + 2 * SUB:q + 3 * SUB],
                                     tbig[:, q + 3 * SUB:q + 4 * SUB],
                                     OP.add))
            vfence()
            for s in range(NSUB):
                cs = slice(s * SUB, (s + 1) * SUB)
                emit("ve", lambda e, c=c, cs=cs, d=2048 + s * SUB:
                     e.tensor_tensor_scan(
                         S_re[:, cs], ones_t[:, :], tbig[:, d:d + SUB],
                         coef[:, C_IRE + c:C_IRE + c + 1], OP.mult, OP.add))
                emit("ve", lambda e, c=c, cs=cs, d=2560 + s * SUB:
                     e.tensor_tensor_scan(
                         S_im[:, cs], ones_t[:, :], tbig[:, d:d + SUB],
                         coef[:, C_IIM + c:C_IIM + c + 1], OP.mult, OP.add))
                vfence()
                send = s * SUB + SUB - 1
                emit("ve", lambda e, c=c, send=send: e.tensor_tensor(
                    coef[:, C_Q1:C_Q1 + 1],
                    coef[:, C_WRE + c:C_WRE + c + 1],
                    S_re[:, send:send + 1], OP.mult))
                emit("ve", lambda e, c=c, send=send: e.tensor_tensor(
                    coef[:, C_Q2:C_Q2 + 1],
                    coef[:, C_WIM + c:C_WIM + c + 1],
                    S_im[:, send:send + 1], OP.mult))
                emit("ve", lambda e, c=c, send=send: e.tensor_tensor(
                    coef[:, C_Q3:C_Q3 + 1],
                    coef[:, C_WRE + c:C_WRE + c + 1],
                    S_im[:, send:send + 1], OP.mult))
                emit("ve", lambda e, c=c, send=send: e.tensor_tensor(
                    coef[:, C_Q4:C_Q4 + 1],
                    coef[:, C_WIM + c:C_WIM + c + 1],
                    S_re[:, send:send + 1], OP.mult))
                vfence()
                emit("ve", lambda e, c=c: e.tensor_tensor(
                    coef[:, C_IRE + c:C_IRE + c + 1],
                    coef[:, C_Q1:C_Q1 + 1], coef[:, C_Q2:C_Q2 + 1],
                    OP.subtract))
                emit("ve", lambda e, c=c: e.tensor_tensor(
                    coef[:, C_IIM + c:C_IIM + c + 1],
                    coef[:, C_Q3:C_Q3 + 1], coef[:, C_Q4:C_Q4 + 1], OP.add))
                vfence()
            for s in range(NSUB):
                cs = slice(s * SUB, (s + 1) * SUB)
                for i, (r3, Sb) in enumerate(((Rn_im3, S_im), (Rn_re3, S_re),
                                              (Rn_re3, S_im), (Rn_im3, S_re))):
                    q = (s * 4 + i) * SUB
                    emit("ve", lambda e, c=c, cs=cs, r3=r3, Sb=Sb, q=q:
                         e.tensor_tensor(tbig[:, q:q + SUB], r3[:, c, :],
                                         Sb[:, cs], OP.mult))
            vfence()
            for s in range(NSUB):
                q = s * 4 * SUB
                emit("ve", lambda e, q=q, d=2048 + s * SUB:
                     e.tensor_tensor(tbig[:, d:d + SUB], tbig[:, q:q + SUB],
                                     tbig[:, q + SUB:q + 2 * SUB],
                                     OP.subtract))
                emit("ve", lambda e, q=q, d=2560 + s * SUB:
                     e.tensor_tensor(tbig[:, d:d + SUB],
                                     tbig[:, q + 2 * SUB:q + 3 * SUB],
                                     tbig[:, q + 3 * SUB:q + 4 * SUB],
                                     OP.add))
            vfence()
            for s in range(NSUB):
                cs = slice(s * SUB, (s + 1) * SUB)
                emit("ve", lambda e, c=c, cs=cs, d=2048 + s * SUB:
                     e.tensor_copy(hsrh3[:, c, cs], tbig[:, d:d + SUB]))
                emit("ve", lambda e, c=c, cs=cs, d=2560 + s * SUB:
                     e.tensor_copy(hsmh3[:, c, cs], tbig[:, d:d + SUB]))
            vfence()
            for s in range(NSUB):
                cs = slice(s * SUB, (s + 1) * SUB)
                emit("ve", lambda e, c=c, cs=cs, d=2048 + s * SUB:
                     e.tensor_tensor(hsrl3[:, c, cs], tbig[:, d:d + SUB],
                                     hsrh3[:, c, cs], OP.subtract))
                last = s == NSUB - 1
                if last:
                    emit("ve", lambda e, c=c, cs=cs, d=2560 + s * SUB:
                         e.tensor_tensor(hsml3[:, c, cs], tbig[:, d:d + SUB],
                                         hsmh3[:, c, cs], OP.subtract)
                         .then_inc(s_ve.h, 1))
                else:
                    emit("ve", lambda e, c=c, cs=cs, d=2560 + s * SUB:
                         e.tensor_tensor(hsml3[:, c, cs], tbig[:, d:d + SUB],
                                         hsmh3[:, c, cs], OP.subtract))
            hs_mark = s_ve.inc(1)
            hs_marks.append(hs_mark)
            bank_free[pair] = (s_ve, bankrel)
            bank_free[pair + 1] = (s_ve, bankrel)

        claim_banks(list(range(8)), extra=[(s_dsy, chi_mark)])
        for c in range(DC):
            wait("pe", s_ve, hs_marks[c])
            for dd in range(DC):
                st = (c == 0)
                emit("pe", lambda e, c=c, dd=dd, st=st: e.matmul(
                    bank(dd), cp_re3[:, c, dd * 128:(dd + 1) * 128],
                    hsrh3[:, c, :], start=st, stop=False))
                emit("pe", lambda e, c=c, dd=dd: e.matmul(
                    bank(dd), cp_im3[:, c, dd * 128:(dd + 1) * 128],
                    hsmh3[:, c, :], start=False, stop=False))
                emit("pe", lambda e, c=c, dd=dd: e.matmul(
                    bank(dd), cp_re3[:, c, dd * 128:(dd + 1) * 128],
                    hsrl3[:, c, :], start=False, stop=False))
                inc_here = (c == DC - 1 and dd == DC - 1)
                if inc_here:
                    emit("pe", lambda e, c=c, dd=dd: e.matmul(
                        bank(dd), cp_im3[:, c, dd * 128:(dd + 1) * 128],
                        hsml3[:, c, :], start=False, stop=False)
                         .then_inc(s_pe.h, 1))
                else:
                    emit("pe", lambda e, c=c, dd=dd: e.matmul(
                        bank(dd), cp_im3[:, c, dd * 128:(dd + 1) * 128],
                        hsml3[:, c, :], start=False, stop=False))
        pass2_mark = s_pe.inc(1)
        wait("sy", s_pe, pass2_mark)
        waits("sy", [(s_bco, prep_mark["clo"][0]),
                     (s_bco2, prep_mark["clo"][1]), (s_dsy, s_dsy.val)])
        emit("sy", lambda e: e.dma_start(cp_re[:, :], clre)
             .then_inc(s_dsy.h, 16))
        clre_mark = s_dsy.inc(16)
        wait("sy", s_dsy, clre_mark)
        emit("sy", lambda e: e.dma_start(cp_im[:, :], clim)
             .then_inc(s_dsy.h, 16))
        clim_mark = s_dsy.inc(16)
        wait("pe", s_dsy, clre_mark)
        for c in range(DC):
            for dd in range(DC):
                emit("pe", lambda e, c=c, dd=dd: e.matmul(
                    bank(dd), cp_re3[:, c, dd * 128:(dd + 1) * 128],
                    hsrh3[:, c, :], start=False, stop=False))
        wait("pe", s_dsy, clim_mark)
        for c in range(DC):
            for dd in range(DC):
                sp = (c == DC - 1)
                inc_here = sp and dd == DC - 1
                if inc_here:
                    emit("pe", lambda e, c=c, dd=dd: e.matmul(
                        bank(dd), cp_im3[:, c, dd * 128:(dd + 1) * 128],
                        hsmh3[:, c, :], start=False, stop=True)
                         .then_inc(s_pe.h, 1))
                else:
                    emit("pe", lambda e, c=c, dd=dd, sp=sp: e.matmul(
                        bank(dd), cp_im3[:, c, dd * 128:(dd + 1) * 128],
                        hsmh3[:, c, :], start=False, stop=sp))
        att_mark = s_pe.inc(1)

        wait("ve", s_pe, att_mark)
        for dd in range(DC):
            emit("ve", lambda e, dd=dd: e.tensor_tensor(
                x3[:, dd, :], x3[:, dd, :], bank(dd), OP.add))
        vfence()
        for dd in range(DC):
            emit("ve", lambda e, dd=dd: e.scalar_tensor_tensor(
                x3[:, dd, :], xn3[:, dd, :],
                vecs[:, V_DSK + dd:V_DSK + dd + 1], x3[:, dd, :],
                OP.mult, OP.add))
        vfence()
        for dd in range(DC):
            if dd == DC - 1:
                emit("ve", lambda e, dd=dd: e.scalar_tensor_tensor(
                    x3[:, dd, :], xl3[:, dd, :],
                    vecs[:, V_DSK + dd:V_DSK + dd + 1], x3[:, dd, :],
                    OP.mult, OP.add).then_inc(s_ve.h, 1))
            else:
                emit("ve", lambda e, dd=dd: e.scalar_tensor_tensor(
                    x3[:, dd, :], xl3[:, dd, :],
                    vecs[:, V_DSK + dd:V_DSK + dd + 1], x3[:, dd, :],
                    OP.mult, OP.add))
        final_mark = s_ve.inc(1)
        for bk in range(8):
            bank_free[bk] = (s_ve, final_mark)
        last_pe = s_pe.val
        wfree[0] = (s_pe, last_pe)
        wfree[1] = (s_pe, last_pe)
        return (s_ve, final_mark), last_pe

    coef_state["scal"] = emit_coef_scalars()

    x_dep = [(s_dsy, x_mark)]
    x_dep_h0 = None
    for tile in range(NT):
        ln1 = emit_ln(x_dep, produce_lo=False, x_dep_h0=x_dep_h0)
        xm, ffn_pe, _m5h, m4_pe = emit_ffn(ln1, tile)
        x_dep = [xm]
        x_half_marks = _m5h
        if phase_limit >= 2:
            ln2 = emit_ln(x_dep, produce_lo=True)
            am, att_pe = emit_attn(tile, ln2, ffn_pe, m4_pe)
            x_dep = [am]
        if phase_limit >= 3:
            ln3 = emit_ln(x_dep, produce_lo=False)
            xm2, _, _m5h, _ = emit_ffn(ln3, tile)
            x_dep = [xm2]
            x_half_marks = _m5h
        st_marks = []
        for h in range(2):
            waits("sy", [x_half_marks[h], (s_st, s_st.val)])
            emit("sy", lambda e, t=tile, h=h: e.dma_start(
                outT3[:, 4 * h:4 * h + 4, t * T:(t + 1) * T],
                x3[:, 4 * h:4 * h + 4, :]).then_inc(s_st.h, 16))
            st_marks.append(s_st.inc(16))
        if tile + 1 < NT:
            for h in range(2):
                waits("gp", [(s_st, st_marks[h]), (s_ld, s_ld.val)])
                emit("gp", lambda e, t=tile + 1, h=h: e.dma_start(
                    x3[:, 4 * h:4 * h + 4, :],
                    xT3[:, 4 * h:4 * h + 4, t * T:(t + 1) * T])
                     .then_inc(s_ld.h, 16))
                s_ld.inc(16)
            x_dep = [(s_ld, s_ld.val)]
            x_dep_h0 = [(s_ld, s_ld.val - 16)]

    wait("sy", s_dsy, s_dsy.val)
    wait("sy", s_st, s_st.val)
    wait("sy", s_ld, s_ld.val)
    wait("sy", s_wA, s_wA.val)
    wait("sy", s_wB, s_wB.val)
    wait("sy", s_gA, s_gA.val)
    wait("sy", s_gB, s_gB.val)
    wait("sy", s_bco, s_bco.val)
    wait("sy", s_bco2, s_bco2.val)
    wait("sy", s_dgp, s_dgp.val)

    with nc.Block() as block:
        @block.tensor
        def _(e):
            for f in Q["pe"]:
                f(e)

        @block.vector
        def _(e):
            for f in Q["ve"]:
                f(e)

        @block.scalar
        def _(e):
            for f in Q["ac"]:
                f(e)

        @block.gpsimd
        def _(e):
            for f in Q["gp"]:
                f(e)

        @block.sync
        def _(e):
            for f in Q["sy"]:
                f(e)

    ctx.close()
    return nc


_NC_CACHE = {}


def kernel(L0_attn_inputs, L0_attn_nu_log, L0_attn_theta_log, L0_attn_B_re,
           L0_attn_B_im, L0_attn_C_re, L0_attn_C_im, L0_attn_D_skip,
           L0_ffn_w1_w, L0_ffn_w1_s, L0_ffn_w2_w, L0_ffn_w2_s, L0_ffn_h1_w,
           L0_ffn_h1_s, L0_ffn_h2_w, L0_ffn_h2_s, L0_ffn_w3_w, L0_ffn_w3_s):
    f32 = np.float32

    def t(a):
        return np.ascontiguousarray(np.asarray(a, dtype=f32).T)

    def pc(v, c):
        return np.ascontiguousarray(np.asarray(v, dtype=f32).reshape(c, 128).T)

    def wpanel(a, nog, kc):
        a = np.asarray(a, dtype=f32)
        return np.ascontiguousarray(
            a.reshape(nog, 512, kc, 128).transpose(0, 3, 2, 1)
            .reshape(nog, 128, kc * 512))

    def bcpanel(a):
        a = np.asarray(a, dtype=f32)
        return np.ascontiguousarray(
            a.T.reshape(DC, 128, a.shape[0]).transpose(1, 0, 2)
            .reshape(128, DC * a.shape[0]))

    shared = {
        "w1T": wpanel(L0_ffn_w1_w, 8, DC), "w2T": wpanel(L0_ffn_w2_w, 8, DC),
        "h1T": wpanel(L0_ffn_h1_w, 8, HC), "h2T": wpanel(L0_ffn_h2_w, 8, HC),
        "w3T": wpanel(L0_ffn_w3_w, 2, HC),
        "breT": bcpanel(L0_attn_B_re), "bimT": bcpanel(L0_attn_B_im),
        "creT": bcpanel(L0_attn_C_re), "cimT": bcpanel(L0_attn_C_im),
        "s1pc": pc(L0_ffn_w1_s, HC), "s2pc": pc(L0_ffn_w2_s, HC),
        "h1spc": pc(L0_ffn_h1_s, HC), "h2spc": pc(L0_ffn_h2_s, HC),
        "w3spc": pc(L0_ffn_w3_s, DC), "dskpc": pc(L0_attn_D_skip, DC),
        "nulpc": pc(L0_attn_nu_log, DC), "thlpc": pc(L0_attn_theta_log, DC),
    }
    x = np.asarray(L0_attn_inputs, dtype=f32)
    in_maps = []
    for b in range(NCORES):
        m = dict(shared)
        m["xT"] = np.ascontiguousarray(x[b].T)
        in_maps.append(m)

    if "nc" not in _NC_CACHE:
        _NC_CACHE["nc"] = build()
    res = run_bass_kernel_spmd(_NC_CACHE["nc"], in_maps,
                               core_ids=list(range(NCORES)))
    out = np.stack([np.ascontiguousarray(r["outT"].T) for r in res.results])
    return out.astype(np.float32)


# version 17
# speedup vs baseline: 1.1650x; 1.0063x over previous
import numpy as np
import contextlib
import concourse.bass as bass
import concourse.mybir as mybir
from concourse.bass_utils import run_bass_kernel_spmd
from concourse.alu_op_type import AluOpType as OP

F32 = mybir.dt.float32
BF16 = mybir.dt.bfloat16
AF = mybir.ActivationFunctionType

NCORES = 8
D = 1024
DC = D // 128
H = 4096
HC = H // 128
T = 512
NT = 2048 // T
SUB = 128
NSUB = T // SUB
OG = 4
EPS_LN = 1e-5
EPS_REF = 1e-5


class Sem:
    def __init__(self, handle):
        self.h = handle
        self.val = 0

    def inc(self, n=1):
        self.val += n
        return self.val


def build(phase_limit=3):
    nc = bass.Bass("TRN2", target_bir_lowering=False)

    def din(name, shape):
        return nc.dram_tensor(name, list(shape), F32, kind="ExternalInput").ap()

    xT = din("xT", (D, 2048))
    w1T = din("w1T", (8, 128, DC * 512)); w2T = din("w2T", (8, 128, DC * 512))
    h1T = din("h1T", (8, 128, HC * 512)); h2T = din("h2T", (8, 128, HC * 512))
    w3T = din("w3T", (2, 128, HC * 512))
    breT = din("breT", (128, DC * D)); bimT = din("bimT", (128, DC * D))
    creT = din("creT", (128, DC * D)); cimT = din("cimT", (128, DC * D))
    s1pc = din("s1pc", (128, HC)); s2pc = din("s2pc", (128, HC))
    h1spc = din("h1spc", (128, HC)); h2spc = din("h2spc", (128, HC))
    w3spc = din("w3spc", (128, DC)); dskpc = din("dskpc", (128, DC))
    nulpc = din("nulpc", (128, DC)); thlpc = din("thlpc", (128, DC))
    outT = nc.dram_tensor("outT", [D, 2048], F32, kind="ExternalOutput").ap()

    def dscr(name, shape):
        return nc.dram_tensor(name, list(shape), BF16, kind="Internal").ap()

    w1S = dscr("w1S", (8, 128, DC * 512)); w2S = dscr("w2S", (8, 128, DC * 512))
    h1S = dscr("h1S", (8, 128, HC * 512)); h2S = dscr("h2S", (8, 128, HC * 512))
    w3S = dscr("w3S", (2, 128, HC * 512))
    bhre = dscr("bhre", (128, DC * D)); bhim = dscr("bhim", (128, DC * D))
    blre = dscr("blre", (128, DC * D)); blim = dscr("blim", (128, DC * D))
    chre = dscr("chre", (128, DC * D)); chim = dscr("chim", (128, DC * D))
    clre = dscr("clre", (128, DC * D)); clim = dscr("clim", (128, DC * D))

    ctx = contextlib.ExitStack()

    L = {}
    _off = [0]

    def reg(name, nbytes):
        L[name] = _off[0]
        _off[0] += (nbytes + 31) // 32 * 32

    reg("x_t", DC * T * 4)
    reg("xn_t", DC * T * 2)
    reg("xn_lo", DC * T * 2)
    reg("tbig", DC * T * 4)
    reg("A_re", DC * SUB * 4); reg("A_im", DC * SUB * 4)
    reg("Rn_re", DC * SUB * 4); reg("Rn_im", DC * SUB * 4)
    reg("S_re", T * 4); reg("S_im", T * 4)
    reg("ones_t", SUB * 4)
    reg("consts", 64 * 4)
    reg("vecs", 192 * 4)
    reg("coef", 224 * 4)
    reg("prep", 8192)
    reg("u_t", HC * T * 2)
    reg("g_t", HC * T * 2)
    reg("wpanA", HC * T * 2)
    reg("wpanB", HC * T * 2)
    total = _off[0]
    assert total <= 212992, f"SBUF over budget: {total}"

    pool_h = ctx.enter_context(
        nc.sbuf_tensor("pool", [128, total], mybir.dt.uint8))
    pool_base = nc.lookup_mloc(pool_h).addr

    def alias(name, shape, dtype, at, extra=0):
        vt = nc.virtual_sbuf_tensor(name, list(shape), dtype)
        nc._tensor(vt.name, list(shape), dtype, type="SB")
        mloc = nc.lookup_mloc(vt)
        mloc.allocated = True
        mloc.addr = pool_base + L[at] + extra
        return vt

    x_t = alias("x_t", [128, DC * T], F32, "x_t")
    xn_t = alias("xn_t", [128, DC * T], BF16, "xn_t")
    xn_lo = alias("xn_lo", [128, DC * T], BF16, "xn_lo")
    tbig = alias("tbig", [128, DC * T], F32, "tbig")
    A_re = alias("A_re", [128, DC * SUB], F32, "A_re")
    A_im = alias("A_im", [128, DC * SUB], F32, "A_im")
    Rn_re = alias("Rn_re", [128, DC * SUB], F32, "Rn_re")
    Rn_im = alias("Rn_im", [128, DC * SUB], F32, "Rn_im")
    S_re = alias("S_re", [128, T], F32, "S_re")
    S_im = alias("S_im", [128, T], F32, "S_im")
    ones_t = alias("ones_t", [128, SUB], F32, "ones_t")
    consts = alias("consts", [128, 64], F32, "consts")
    vecs = alias("vecs", [128, 192], F32, "vecs")
    coef = alias("coef", [128, 224], F32, "coef")
    prep_f = [alias("prep_f0", [128, 512], F32, "prep"),
              alias("prep_f1", [128, 512], F32, "prep", 4096)]
    prep_hi = [alias("prep_h0", [128, 512], BF16, "prep", 2048),
               alias("prep_h1", [128, 512], BF16, "prep", 6144)]
    prep_lo = [alias("prep_l0", [128, 512], BF16, "prep", 3072),
               alias("prep_l1", [128, 512], BF16, "prep", 7168)]
    u_t = alias("u_t", [128, HC * T], BF16, "u_t")
    g_t = alias("g_t", [128, HC * T], BF16, "g_t")
    wpan = [alias("wpanA", [128, HC * T], BF16, "wpanA"),
            alias("wpanB", [128, HC * T], BF16, "wpanB")]
    bh_re = alias("bh_re", [128, DC * D], BF16, "u_t")
    bh_im = alias("bh_im", [128, DC * D], BF16, "u_t", DC * D * 2)
    bl_re = alias("bl_re", [128, DC * D], BF16, "g_t")
    bl_im = alias("bl_im", [128, DC * D], BF16, "g_t", DC * D * 2)
    cp_re = alias("cp_re", [128, DC * D], BF16, "wpanA")
    cp_im = alias("cp_im", [128, DC * D], BF16, "wpanA", DC * D * 2)
    hsrh = alias("hsrh", [128, DC * T], BF16, "wpanB")
    hsrl = alias("hsrl", [128, DC * T], BF16, "wpanB", DC * T * 2)
    hsmh = alias("hsmh", [128, DC * T], BF16, "wpanB", DC * T * 4)
    hsml = alias("hsml", [128, DC * T], BF16, "wpanB", DC * T * 6)

    K_ONECOL, K_PIHALF, K_EPSLN, K_FENCE = 0, 1, 2, 3
    V_S1, V_S2, V_H1S, V_H2S = 0, 32, 64, 96
    V_W3S, V_HALFW3S, V_DSK, V_NUL, V_THL = 128, 136, 144, 152, 160
    C_NU, C_TH, C_EMN, C_EPN = 0, 8, 16, 24
    C_SIN, C_COS, C_GAM = 32, 40, 48
    C_LRE, C_LIM, C_QRE, C_QIM = 56, 64, 72, 80
    C_CT1, C_CT2, C_CT3, C_CT4 = 88, 96, 104, 112
    C_IRE, C_IIM, C_WRE, C_WIM = 120, 128, 136, 144
    C_Q1, C_Q2, C_Q3, C_Q4 = 152, 153, 154, 155
    R_NEGMU, R_M2, R_TMP, R_VAR, R_RSTD, R_NMR = 0, 512, 1024, 1536, 2048, 2560
    LADDER = 3072
    SQ0, SQ1 = 3072, 3584
    LQRE, LQIM, LPRE, LPIM = LADDER, LADDER + 64, LADDER + 128, LADDER + 192

    psum = ctx.enter_context(nc.psum_tensor("ps", [128, 4096], F32))

    def bank(i, w=T):
        return psum[:, i * 512:i * 512 + w]

    def bank_row(i, w=T):
        return psum[0:1, i * 512:i * 512 + w]

    def bank_sub(i, s):
        return psum[:, i * 512 + s * SUB:i * 512 + (s + 1) * SUB]

    s_pe = Sem(ctx.enter_context(nc.semaphore("s_pe")))
    s_ve = Sem(ctx.enter_context(nc.semaphore("s_ve")))
    s_ac = Sem(ctx.enter_context(nc.semaphore("s_ac")))
    s_dsy = Sem(ctx.enter_context(nc.semaphore("s_dsy")))
    s_dgp = Sem(ctx.enter_context(nc.semaphore("s_dgp")))
    s_po = Sem(ctx.enter_context(nc.semaphore("s_po")))
    s_bco = Sem(ctx.enter_context(nc.semaphore("s_bco")))
    s_bco2 = Sem(ctx.enter_context(nc.semaphore("s_bco2")))
    s_wA = Sem(ctx.enter_context(nc.semaphore("s_wA")))
    s_wB = Sem(ctx.enter_context(nc.semaphore("s_wB")))
    s_gA = Sem(ctx.enter_context(nc.semaphore("s_gA")))
    s_gB = Sem(ctx.enter_context(nc.semaphore("s_gB")))
    s_st = Sem(ctx.enter_context(nc.semaphore("s_st")))
    s_dac = Sem(ctx.enter_context(nc.semaphore("s_dac")))
    s_ld = Sem(ctx.enter_context(nc.semaphore("s_ld")))

    Q = {"pe": [], "ve": [], "ac": [], "gp": [], "sy": []}

    def emit(q, fn):
        Q[q].append(fn)

    def wait(q, sem, val):
        if sem is not None and val > 0:
            emit(q, lambda e, s=sem.h, v=val: e.wait_ge(s, v))

    def waits(q, pairs):
        best = {}
        for p in pairs:
            if p is None:
                continue
            sem, val = p
            if sem is None or val <= 0:
                continue
            k = id(sem.h)
            if k not in best or best[k][1] < val:
                best[k] = (sem, val)
        for sem, val in best.values():
            wait(q, sem, val)

    def vfence():
        emit("ve", lambda e: e.memset(consts[:, K_FENCE:K_FENCE + 1], 0.0)
             .then_inc(s_ve.h, 1))
        wait("ve", s_ve, s_ve.inc(1))

    bank_free = [None] * 8
    wfree = [None, None]
    parity = [0]

    def claim_banks(banks, extra=()):
        waits("pe", [bank_free[i] for i in banks] + list(extra))

    prep_mark = {}

    def emit_casts(stages):
        for name, dst, srcw, nog in stages:
            for og in range(nog):
                emit("gp", lambda e, d=dst[og], s=srcw[og]:
                     e.dma_start(d, s).then_inc(s_dgp.h, 16))
                wait("gp", s_dgp, s_dgp.inc(16))
                prep_mark[(name, og)] = s_dgp.val

    emit_casts((("w1S", w1S, w1T, 8), ("w2S", w2S, w2T, 8),
                ("w3S", w3S, w3T, 2)))
    for name, dst, srcw in (("bhre", bhre, breT), ("bhim", bhim, bimT),
                            ("chre", chre, creT), ("chim", chim, cimT)):
        emit("gp", lambda e, d=dst, s=srcw:
             e.dma_start(d, s).then_inc(s_dgp.h, 16))
        wait("gp", s_dgp, s_dgp.inc(16))
        prep_mark[name] = s_dgp.val

    def ldvec(dst_off, width, src):
        emit("sy", lambda e, d=vecs[:, dst_off:dst_off + width], s=src:
             e.dma_start(d, s).then_inc(s_dsy.h, 16))
        return s_dsy.inc(16)

    ldvec(V_S1, HC, s1pc); ldvec(V_S2, HC, s2pc)
    ldvec(V_H1S, HC, h1spc); ldvec(V_H2S, HC, h2spc)
    ldvec(V_W3S, DC, w3spc); ldvec(V_DSK, DC, dskpc)
    ldvec(V_NUL, DC, nulpc)
    vec_mark = ldvec(V_THL, DC, thlpc)

    xT3 = xT.rearrange("(c p) t -> p c t", p=128)
    outT3 = outT.rearrange("(c p) t -> p c t", p=128)
    x3 = x_t.ap().rearrange("p (c t) -> p c t", t=T)
    xn3 = xn_t.ap().rearrange("p (c t) -> p c t", t=T)
    xl3 = xn_lo.ap().rearrange("p (c t) -> p c t", t=T)
    tb3 = tbig.ap().rearrange("p (c t) -> p c t", t=T)

    def emit_x_load(tile, dep):
        waits("sy", [dep])
        wait("sy", s_dsy, s_dsy.val)
        emit("sy", lambda e, t=tile: e.dma_start(
            x3, xT3[:, :, t * T:(t + 1) * T]).then_inc(s_dsy.h, 16))
        return s_dsy.inc(16)

    x_mark = emit_x_load(0, None)

    emit("ve", lambda e: e.memset(ones_t[:, :], 1.0))
    emit("ve", lambda e: e.memset(consts[:, K_ONECOL:K_ONECOL + 1], 1.0))
    emit("ve", lambda e: e.memset(consts[:, K_PIHALF:K_PIHALF + 1],
                                  float(np.pi / 2)))
    emit("ve", lambda e: e.memset(consts[:, K_EPSLN:K_EPSLN + 1], float(EPS_LN)))
    emit("ve", lambda e: e.memset(coef[:, C_IRE:C_IRE + DC], 0.0))
    emit("ve", lambda e: e.memset(coef[:, C_IIM:C_IIM + DC], 0.0)
         .then_inc(s_ve.h, 1))
    consts_mark = s_ve.inc(1)

    wait("ve", s_dsy, vec_mark)
    emit("ve", lambda e: e.tensor_scalar(
        vecs[:, V_HALFW3S:V_HALFW3S + DC], vecs[:, V_W3S:V_W3S + DC],
        0.5, None, OP.mult).then_inc(s_ve.h, 1))
    s_ve.inc(1)

    def cc(i, w=DC):
        return coef[:, i:i + w]

    def emit_coef_scalars():
        waits("ac", [(s_dsy, vec_mark), (s_ve, consts_mark)])
        emit("ac", lambda e: e.activation(cc(C_NU), vecs[:, V_NUL:V_NUL + DC],
                                          AF.Exp))
        emit("ac", lambda e: e.activation(cc(C_TH), vecs[:, V_THL:V_THL + DC],
                                          AF.Exp).then_inc(s_ac.h, 1))
        wait("ac", s_ac, s_ac.inc(1))
        emit("ac", lambda e: e.activation(cc(C_EMN), cc(C_NU), AF.Exp,
                                          scale=-1.0))
        emit("ac", lambda e: e.activation(cc(C_EPN), cc(C_NU), AF.Exp))
        emit("ac", lambda e: e.activation(cc(C_SIN), cc(C_TH), AF.Sin))
        emit("ac", lambda e: e.activation(cc(C_COS), cc(C_TH), AF.Sin,
                                          scale=-1.0,
                                          bias=consts[:, K_PIHALF:K_PIHALF + 1]))
        emit("ac", lambda e: e.activation(cc(C_CT1), cc(C_NU), AF.Exp,
                                          scale=-2.0).then_inc(s_ac.h, 1))
        m1 = s_ac.inc(1)
        wait("ve", s_ac, m1)
        emit("ve", lambda e: e.tensor_scalar(cc(C_CT2), cc(C_CT1), -1.0,
                                             1.0 + EPS_REF, OP.mult, OP.add)
             .then_inc(s_ve.h, 1))
        m2 = s_ve.inc(1)
        wait("ac", s_ve, m2)
        emit("ac", lambda e: e.activation(cc(C_CT3), cc(C_CT2), AF.Ln)
             .then_inc(s_ac.h, 1))
        wait("ac", s_ac, s_ac.inc(1))
        emit("ac", lambda e: e.activation(cc(C_GAM), cc(C_CT3), AF.Exp,
                                          scale=0.5).then_inc(s_ac.h, 1))
        return s_ac.inc(1)

    def emit_bc_prep():
        chunks = []
        for srcT, hiS, loS in ((breT, bhre, blre), (bimT, bhim, blim),
                               (creT, chre, clre), (cimT, chim, clim)):
            for j in range(16):
                cs = slice(j * 512, (j + 1) * 512)
                chunks.append((srcT, hiS, loS, cs))
        in_marks = [None] * len(chunks)

        def issue_in(i):
            srcT, hiS, loS, cs = chunks[i]
            buf = i % 2
            emit("gp", lambda e, s=srcT[:, cs], b=buf:
                 e.dma_start(prep_f[b][:, :], s).then_inc(s_dgp.h, 16))
            s_dgp.inc(16)
            emit("gp", lambda e, s=hiS[:, cs], b=buf:
                 e.dma_start(prep_hi[b][:, :], s).then_inc(s_dgp.h, 16))
            in_marks[i] = s_dgp.inc(16)

        issue_in(0)
        out_marks = [0] * len(chunks)
        for i in range(len(chunks)):
            srcT, hiS, loS, cs = chunks[i]
            buf = i % 2
            wait("gp", s_dgp, in_marks[i])
            if i + 1 < len(chunks):
                issue_in(i + 1)
            osem = s_bco if buf == 0 else s_bco2
            if i >= 2:
                wait("gp", osem, out_marks[i - 2])
            emit("gp", lambda e, b=buf: e.tensor_tensor(
                prep_lo[b][:, :], prep_f[b][:, :], prep_hi[b][:, :],
                OP.subtract).then_inc(s_po.h, 1))
            wait("gp", s_po, s_po.inc(1))
            emit("gp", lambda e, d=loS[:, cs], b=buf, sm=osem.h:
                 e.dma_start(d, prep_lo[b][:, :]).then_inc(sm, 16))
            out_marks[i] = osem.inc(16)
            if i == 31:
                prep_mark["blo"] = (s_bco.val, s_bco2.val)
        prep_mark["clo"] = (s_bco.val, s_bco2.val)

    def emit_coef_arrays(scal_mark):
        wait("ve", s_ac, scal_mark)
        rows = tbig
        emit("ve", lambda e: e.tensor_tensor(cc(C_LRE), cc(C_EMN), cc(C_COS),
                                             OP.mult))
        emit("ve", lambda e: e.tensor_tensor(cc(C_LIM), cc(C_EMN), cc(C_SIN),
                                             OP.mult))
        emit("ve", lambda e: e.tensor_tensor(cc(C_QRE), cc(C_EPN), cc(C_COS),
                                             OP.mult))
        emit("ve", lambda e: e.scalar_tensor_tensor(cc(C_QIM), cc(C_EPN), -1.0,
                                                    cc(C_SIN), OP.mult,
                                                    OP.mult))
        vfence()
        emit("ve", lambda e: e.tensor_copy(rows[:, LQRE:LQRE + DC], cc(C_QRE)))
        emit("ve", lambda e: e.tensor_copy(rows[:, LQIM:LQIM + DC], cc(C_QIM)))
        emit("ve", lambda e: e.tensor_copy(rows[:, LPRE:LPRE + DC], cc(C_LRE)))
        emit("ve", lambda e: e.tensor_copy(rows[:, LPIM:LPIM + DC], cc(C_LIM)))
        vfence()
        for k in range(7):
            qre = rows[:, LQRE + k * DC:LQRE + (k + 1) * DC]
            qim = rows[:, LQIM + k * DC:LQIM + (k + 1) * DC]
            pre = rows[:, LPRE + k * DC:LPRE + (k + 1) * DC]
            pim = rows[:, LPIM + k * DC:LPIM + (k + 1) * DC]
            emit("ve", lambda e, a=qre: e.tensor_tensor(cc(C_CT1), a, a,
                                                        OP.mult))
            emit("ve", lambda e, a=qim: e.tensor_tensor(cc(C_CT2), a, a,
                                                        OP.mult))
            emit("ve", lambda e, a=pre: e.tensor_tensor(cc(C_CT3), a, a,
                                                        OP.mult))
            emit("ve", lambda e, a=pim: e.tensor_tensor(cc(C_CT4), a, a,
                                                        OP.mult))
            vfence()
            emit("ve", lambda e, a=qre, b=qim,
                 d=rows[:, LQIM + (k + 1) * DC:LQIM + (k + 2) * DC]:
                 e.scalar_tensor_tensor(d, a, 2.0, b, OP.mult, OP.mult))
            emit("ve", lambda e, a=pre, b=pim,
                 d=rows[:, LPIM + (k + 1) * DC:LPIM + (k + 2) * DC]:
                 e.scalar_tensor_tensor(d, a, 2.0, b, OP.mult, OP.mult))
            emit("ve", lambda e,
                 d=rows[:, LQRE + (k + 1) * DC:LQRE + (k + 2) * DC]:
                 e.tensor_tensor(d, cc(C_CT1), cc(C_CT2), OP.subtract))
            emit("ve", lambda e,
                 d=rows[:, LPRE + (k + 1) * DC:LPRE + (k + 2) * DC]:
                 e.tensor_tensor(d, cc(C_CT3), cc(C_CT4), OP.subtract))
            vfence()
        emit("ve", lambda e: e.tensor_copy(
            cc(C_WRE), rows[:, LPRE + 7 * DC:LPRE + 8 * DC]))
        emit("ve", lambda e: e.tensor_copy(
            cc(C_WIM), rows[:, LPIM + 7 * DC:LPIM + 8 * DC]))
        vfence()

        st_re = [tbig[:, c * SUB:(c + 1) * SUB] for c in range(DC)]
        st_im = [tbig[:, 1024 + c * SUB:1024 + (c + 1) * SUB]
                 for c in range(DC)]
        tmpu = [tbig[:, 2048 + i * SUB:2048 + (i + 1) * SUB] for i in range(4)]
        tmpv = [tbig[:, 2560 + i * SUB:2560 + (i + 1) * SUB] for i in range(4)]

        def build_arr(dst_re, dst_im, lre, lim, base_kind):
            for c in range(DC):
                if base_kind == "gamma":
                    emit("ve", lambda e, c=c: e.tensor_copy(
                        st_re[c][:, 0:1], coef[:, C_GAM + c:C_GAM + c + 1]))
                else:
                    emit("ve", lambda e, c=c: e.memset(st_re[c][:, 0:1], -1.0))
                emit("ve", lambda e, c=c: e.memset(st_im[c][:, 0:1], 0.0))
            vfence()
            n = 1
            for k in range(7):
                for g0 in range(0, DC, 4):
                    for i, c in enumerate(range(g0, g0 + 4)):
                        pr = rows[:, lre + k * DC + c:lre + k * DC + c + 1]
                        pi_ = rows[:, lim + k * DC + c:lim + k * DC + c + 1]
                        emit("ve", lambda e, c=c, pi_=pi_, u=tmpu[i], n=n:
                             e.tensor_scalar(u[:, 0:n], st_im[c][:, 0:n],
                                             pi_, None, OP.mult))
                        emit("ve", lambda e, c=c, pr=pr, v=tmpv[i], n=n:
                             e.tensor_scalar(v[:, 0:n], st_im[c][:, 0:n],
                                             pr, None, OP.mult))
                    vfence()
                    for i, c in enumerate(range(g0, g0 + 4)):
                        pr = rows[:, lre + k * DC + c:lre + k * DC + c + 1]
                        pi_ = rows[:, lim + k * DC + c:lim + k * DC + c + 1]
                        emit("ve", lambda e, c=c, pr=pr, u=tmpu[i], n=n:
                             e.scalar_tensor_tensor(
                                 st_re[c][:, n:2 * n], st_re[c][:, 0:n], pr,
                                 u[:, 0:n], OP.mult, OP.subtract))
                        emit("ve", lambda e, c=c, pi_=pi_, v=tmpv[i], n=n:
                             e.scalar_tensor_tensor(
                                 st_im[c][:, n:2 * n], st_re[c][:, 0:n], pi_,
                                 v[:, 0:n], OP.mult, OP.add))
                    vfence()
                n *= 2
            emit("ve", lambda e, d=dst_re: e.tensor_copy(
                d[:, :], tbig[:, 0:DC * SUB]))
            emit("ve", lambda e, d=dst_im: e.tensor_copy(
                d[:, :], tbig[:, 1024:1024 + DC * SUB]))
            vfence()

        build_arr(A_re, A_im, LQRE, LQIM, "gamma")
        build_arr(Rn_re, Rn_im, LPRE, LPIM, "neg1")
        emit("ve", lambda e: e.memset(consts[:, K_FENCE:K_FENCE + 1], 0.0)
             .then_inc(s_ve.h, 1))
        return s_ve.inc(1)

    def row(o, w=T):
        return tbig[0:1, o:o + w]

    def emit_ln(x_dep_pairs, produce_lo, x_dep_h0=None):
        h0 = x_dep_h0 if x_dep_h0 is not None else x_dep_pairs
        claim_banks([0, 1], extra=list(h0) + [(s_ve, consts_mark)])
        for dk in range(DC):
            if dk == 4:
                waits("pe", x_dep_pairs)
            emit("pe", lambda e, dk=dk: e.matmul(
                bank_row(0), consts[:, K_ONECOL:K_ONECOL + 1], x3[:, dk, :],
                start=(dk == 0), stop=(dk == DC - 1)))
        waits("ac", h0)
        sq = [tbig[:, SQ0:SQ0 + T], tbig[:, SQ1:SQ1 + T]]
        sq_ac, pe_sq = [], []
        for dk in range(DC):
            dst = sq[dk % 2]
            if dk == 4:
                waits("ac", x_dep_pairs)
            if dk >= 2:
                wait("ac", s_pe, pe_sq[dk - 2])
            emit("ac", lambda e, dst=dst, dk=dk: e.activation(
                dst, x3[:, dk, :], AF.Square).then_inc(s_ac.h, 1))
            sq_ac.append(s_ac.inc(1))
            wait("pe", s_ac, sq_ac[dk])
            emit("pe", lambda e, src=dst, dk=dk: e.matmul(
                bank_row(1), consts[:, K_ONECOL:K_ONECOL + 1], src,
                start=(dk == 0), stop=(dk == DC - 1)).then_inc(s_pe.h, 1))
            pe_sq.append(s_pe.inc(1))
        stats_mark = pe_sq[-1]

        wait("ve", s_pe, stats_mark)
        emit("ve", lambda e: e.tensor_scalar(row(R_NEGMU), bank_row(0),
                                             -1.0 / D, None, OP.mult))
        emit("ve", lambda e: e.tensor_scalar(row(R_M2), bank_row(1),
                                             1.0 / D, None, OP.mult)
             .then_inc(s_ve.h, 1))
        mu_mark = s_ve.inc(1)
        wait("ac", s_ve, mu_mark)
        emit("ac", lambda e: e.activation(row(R_TMP), row(R_NEGMU), AF.Square)
             .then_inc(s_ac.h, 1))
        sqm_mark = s_ac.inc(1)
        wait("ve", s_ac, sqm_mark)
        emit("ve", lambda e: e.tensor_tensor(row(R_VAR), row(R_M2), row(R_TMP),
                                             OP.subtract).then_inc(s_ve.h, 1))
        var_mark = s_ve.inc(1)
        wait("ac", s_ve, var_mark)
        emit("ac", lambda e: e.activation(row(R_RSTD), row(R_VAR), AF.Ln,
                                          bias=consts[0:1, K_EPSLN:K_EPSLN + 1])
             .then_inc(s_ac.h, 1))
        wait("ac", s_ac, s_ac.inc(1))
        emit("ac", lambda e: e.activation(row(R_RSTD), row(R_RSTD), AF.Exp,
                                          scale=-0.5).then_inc(s_ac.h, 1))
        rstd_mark = s_ac.inc(1)
        wait("ve", s_ac, rstd_mark)
        emit("ve", lambda e: e.tensor_tensor(row(R_NMR), row(R_NEGMU),
                                             row(R_RSTD), OP.mult)
             .then_inc(s_ve.h, 1))
        nmr_mark = s_ve.inc(1)
        claim_banks([2, 3], extra=[(s_ve, nmr_mark)])
        emit("pe", lambda e: e.matmul(bank(2), ones_t[0:1, 0:128], row(R_RSTD),
                                      start=True, stop=True))
        emit("pe", lambda e: e.matmul(bank(3), ones_t[0:1, 0:128], row(R_NMR),
                                      start=True, stop=True)
             .then_inc(s_pe.h, 1))
        bc_mark = s_pe.inc(1)
        wait("ve", s_pe, bc_mark)
        for dk in range(DC):
            emit("ve", lambda e, dk=dk: e.tensor_tensor(
                tb3[:, dk, :], x3[:, dk, :], bank(2), OP.mult))
        vfence()
        for dk in range(DC):
            last = (dk == DC - 1) and not produce_lo
            if last:
                emit("ve", lambda e, dk=dk: e.tensor_tensor(
                    xn3[:, dk, :], tb3[:, dk, :], bank(3), OP.add)
                     .then_inc(s_ve.h, 1))
            else:
                emit("ve", lambda e, dk=dk: e.tensor_tensor(
                    xn3[:, dk, :], tb3[:, dk, :], bank(3), OP.add))
        if not produce_lo:
            apply_mark = s_ve.inc(1)
        else:
            vfence()
            for dk in range(DC):
                emit("ve", lambda e, dk=dk: e.tensor_tensor(
                    tb3[:, dk, :], tb3[:, dk, :], xn3[:, dk, :], OP.subtract))
            vfence()
            for dk in range(DC):
                if dk == DC - 1:
                    emit("ve", lambda e, dk=dk: e.tensor_tensor(
                        xl3[:, dk, :], tb3[:, dk, :], bank(3), OP.add)
                         .then_inc(s_ve.h, 1))
                else:
                    emit("ve", lambda e, dk=dk: e.tensor_tensor(
                        xl3[:, dk, :], tb3[:, dk, :], bank(3), OP.add))
            apply_mark = s_ve.inc(1)
        bank_free[0] = (s_ve, mu_mark)
        bank_free[1] = (s_ve, mu_mark)
        bank_free[2] = (s_ve, apply_mark)
        bank_free[3] = (s_ve, apply_mark)
        return apply_mark

    def og_load_cast(srcF, KC, og, par):
        sem = s_gA if par == 0 else s_gB
        waits("gp", [wfree[par], (sem, sem.val)])
        emit("gp", lambda e, d=wpan[par][:, 0:KC * 512], s=srcF[og], sm=sem.h:
             e.dma_start(d, s).then_inc(sm, 16))
        sem.inc(16)
        return (sem, sem.val)

    def og_load(scrS, KC, og, par, prepname):
        sem = s_wA if par == 0 else s_wB
        waits("sy", [wfree[par], (s_dgp, prep_mark[(prepname, og)]),
                     (sem, sem.val)])
        emit("sy", lambda e, d=wpan[par][:, 0:KC * 512], s=scrS[og], sm=sem.h:
             e.dma_start(d, s).then_inc(sm, 16))
        sem.inc(16)
        return (sem, sem.val)

    def ffn_stage(scrS, prepname, KC, OC, rhs_t, evac, rhs_dep,
                  cast_src=None):
        rhs3 = rhs_t.ap().rearrange("p (k t) -> p k t", t=T)
        marks = []
        for og in range(OC // OG):
            par = parity[0]
            parity[0] ^= 1
            if cast_src is not None:
                dmark = og_load_cast(cast_src, KC, og, par)
            else:
                dmark = og_load(scrS, KC, og, par, prepname)
            banks = [par * 4 + i for i in range(OG)]
            claim_banks(banks, extra=[dmark] + list(rhs_dep))
            for k in range(KC):
                for oc in range(OG):
                    st, sp = (k == 0), (k == KC - 1)
                    w_ap = wpan[par][:, k * 512 + oc * 128:
                                     k * 512 + oc * 128 + 128]
                    if sp and oc == OG - 1:
                        emit("pe", lambda e, w=w_ap, r=rhs3[:, k, :],
                             o=bank(banks[oc]), st=st:
                             e.matmul(o, w, r, start=st, stop=True)
                             .then_inc(s_pe.h, 1))
                    else:
                        emit("pe", lambda e, w=w_ap, r=rhs3[:, k, :],
                             o=bank(banks[oc]), st=st, sp=sp:
                             e.matmul(o, w, r, start=st, stop=sp))
            grp = s_pe.inc(1)
            wfree[par] = (s_pe, grp)
            ev_mark = evac(og, banks, grp)
            for bk in banks:
                bank_free[bk] = ev_mark
            marks.append((grp, ev_mark))
        return marks

    def evac_act(dst_t, func, svec_off):
        dst3 = dst_t.ap().rearrange("p (k t) -> p k t", t=T)

        def f(og, banks, grp):
            wait("ac", s_pe, grp)
            for i, bk in enumerate(banks):
                ocg = og * OG + i
                sc = vecs[:, svec_off + ocg:svec_off + ocg + 1]
                if i == OG - 1:
                    emit("ac", lambda e, bk=bk, ocg=ocg, sc=sc: e.activation(
                        dst3[:, ocg, :], bank(bk), func, scale=sc)
                         .then_inc(s_ac.h, 1))
                else:
                    emit("ac", lambda e, bk=bk, ocg=ocg, sc=sc: e.activation(
                        dst3[:, ocg, :], bank(bk), func, scale=sc))
            return (s_ac, s_ac.inc(1))
        return f

    def evac_mul(dst_t, a_t, svec_off):
        dst3 = dst_t.ap().rearrange("p (k t) -> p k t", t=T)
        a3 = a_t.ap().rearrange("p (k t) -> p k t", t=T)

        def f(og, banks, grp):
            wait("ve", s_pe, grp)
            for i, bk in enumerate(banks):
                ocg = og * OG + i
                sc = vecs[:, svec_off + ocg:svec_off + ocg + 1]
                if i == OG - 1:
                    emit("ve", lambda e, bk=bk, ocg=ocg, sc=sc:
                         e.scalar_tensor_tensor(dst3[:, ocg, :], bank(bk), sc,
                                                a3[:, ocg, :], OP.mult,
                                                OP.mult).then_inc(s_ve.h, 1))
                else:
                    emit("ve", lambda e, bk=bk, ocg=ocg, sc=sc:
                         e.scalar_tensor_tensor(dst3[:, ocg, :], bank(bk), sc,
                                                a3[:, ocg, :], OP.mult,
                                                OP.mult))
            return (s_ve, s_ve.inc(1))
        return f

    def evac_resid(svec_off):
        def f(og, banks, grp):
            wait("ve", s_pe, grp)
            for i, bk in enumerate(banks):
                ocg = og * OG + i
                sc = vecs[:, svec_off + ocg:svec_off + ocg + 1]
                if i == OG - 1:
                    emit("ve", lambda e, bk=bk, ocg=ocg, sc=sc:
                         e.scalar_tensor_tensor(x3[:, ocg, :], bank(bk), sc,
                                                x3[:, ocg, :], OP.mult,
                                                OP.add).then_inc(s_ve.h, 1))
                else:
                    emit("ve", lambda e, bk=bk, ocg=ocg, sc=sc:
                         e.scalar_tensor_tensor(x3[:, ocg, :], bank(bk), sc,
                                                x3[:, ocg, :], OP.mult,
                                                OP.add))
            return (s_ve, s_ve.inc(1))
        return f

    coef_state = {"pending": True, "scal": None, "mark": None, "hcast": []}

    def emit_ffn(xn_mark, tile):
        dep = [(s_ve, xn_mark)]
        m1 = ffn_stage(w1S, "w1S", DC, HC, xn_t,
                       evac_act(g_t, AF.Silu, V_S1), dep)
        a_done = m1[-1][1]
        m2 = ffn_stage(w2S, "w2S", DC, HC, xn_t,
                       evac_mul(u_t, g_t, V_S2), dep + [a_done])
        u_done = m2[-1][1]
        first = tile == 0 and coef_state["pending"]
        if first:
            coef_state["mark"] = emit_coef_arrays(coef_state["scal"])
            coef_state["pending"] = False
        m3 = ffn_stage(h1S, "h1S", HC, HC, u_t,
                       evac_act(g_t, AF.Gelu, V_H1S), [u_done],
                       cast_src=h1T if first else None)
        g_done = m3[-1][1]
        m4 = ffn_stage(h2S, "h2S", HC, HC, g_t,
                       evac_act(u_t, AF.Gelu, V_H2S), [g_done],
                       cast_src=h2T if first else None)
        g2_done = m4[-1][1]
        m5 = ffn_stage(w3S, "w3S", HC, DC, u_t,
                       evac_resid(V_HALFW3S), [g2_done])
        if first:
            emit_bc_prep()
            coef_state["hcast"] = [("h1S", h1S, h1T, og) for og in range(8)] \
                + [("h2S", h2S, h2T, og) for og in range(8)]
        return m5[-1][1], s_pe.val, [m5[0][1], m5[1][1]], m4[-1][0]

    bh_re3 = bh_re.ap().rearrange("p (k h) -> p k h", h=D)
    bh_im3 = bh_im.ap().rearrange("p (k h) -> p k h", h=D)
    bl_re3 = bl_re.ap().rearrange("p (k h) -> p k h", h=D)
    bl_im3 = bl_im.ap().rearrange("p (k h) -> p k h", h=D)
    cp_re3 = cp_re.ap().rearrange("p (k d) -> p k d", d=D)
    cp_im3 = cp_im.ap().rearrange("p (k d) -> p k d", d=D)
    hsrh3 = hsrh.ap().rearrange("p (c t) -> p c t", t=T)
    hsrl3 = hsrl.ap().rearrange("p (c t) -> p c t", t=T)
    hsmh3 = hsmh.ap().rearrange("p (c t) -> p c t", t=T)
    hsml3 = hsml.ap().rearrange("p (c t) -> p c t", t=T)
    A_re3 = A_re.ap().rearrange("p (c j) -> p c j", j=SUB)
    A_im3 = A_im.ap().rearrange("p (c j) -> p c j", j=SUB)
    Rn_re3 = Rn_re.ap().rearrange("p (c j) -> p c j", j=SUB)
    Rn_im3 = Rn_im.ap().rearrange("p (c j) -> p c j", j=SUB)
    def emit_attn(tile, xn_mark, ffn1_pe_mark, m4_pe_mark):
        waits("sy", [(s_pe, m4_pe_mark), (s_bco, prep_mark["blo"][0]),
                     (s_bco2, prep_mark["blo"][1]), (s_dsy, s_dsy.val)])
        for dt_, s3 in ((bl_re, blre), (bl_im, blim)):
            emit("sy", lambda e, d=dt_, s=s3:
                 e.dma_start(d[:, :], s).then_inc(s_dsy.h, 16))
            s_dsy.inc(16)
        waits("ac", [(s_pe, ffn1_pe_mark), (s_dgp, prep_mark["bhim"]),
                     (s_dac, s_dac.val)])
        for dt_, s3 in ((bh_re, bhre), (bh_im, bhim)):
            emit("ac", lambda e, d=dt_, s=s3:
                 e.dma_start(d[:, :], s).then_inc(s_dac.h, 16))
            s_dac.inc(16)
        bpan_mark = (s_dac.val, s_dsy.val)
        waits("sy", [wfree[0], wfree[1], (s_dgp, prep_mark["chim"]),
                     (s_dsy, s_dsy.val)])
        for dt_, s3 in ((cp_re, chre), (cp_im, chim)):
            emit("sy", lambda e, d=dt_, s=s3:
                 e.dma_start(d[:, :], s).then_inc(s_dsy.h, 16))
            s_dsy.inc(16)
        chi_mark = s_dsy.val

        hs_marks = []
        for c in range(DC):
            pair = (c % 2) * 2
            extra = [(s_dac, bpan_mark[0]), (s_dsy, bpan_mark[1]),
                     (s_ve, xn_mark)]
            if tile == 0 and c == 0 and coef_state["mark"] is not None:
                extra.append((s_ve, coef_state["mark"]))
            claim_banks([pair, pair + 1], extra=extra)
            passes = [(bh_re3, bh_im3, xn3), (bl_re3, bl_im3, xn3),
                      (bh_re3, bh_im3, xl3)]
            for p_idx, (pre3, pim3, rhs3) in enumerate(passes):
                first = (p_idx == 0)
                last = (p_idx == len(passes) - 1)
                for dk in range(DC):
                    emit("pe", lambda e, dk=dk, c=c, p3=pre3, r3=rhs3, pb=pair,
                         st=(first and dk == 0), sp=(last and dk == DC - 1):
                         e.matmul(bank(pb), p3[:, dk, c * 128:(c + 1) * 128],
                                  r3[:, dk, :], start=st, stop=sp))
                for dk in range(DC):
                    st = first and dk == 0
                    sp = last and dk == DC - 1
                    if sp:
                        emit("pe", lambda e, dk=dk, c=c, p3=pim3, r3=rhs3,
                             pb=pair, st=st:
                             e.matmul(bank(pb + 1),
                                      p3[:, dk, c * 128:(c + 1) * 128],
                                      r3[:, dk, :], start=st, stop=True)
                             .then_inc(s_pe.h, 1))
                    else:
                        emit("pe", lambda e, dk=dk, c=c, p3=pim3, r3=rhs3,
                             pb=pair, st=st:
                             e.matmul(bank(pb + 1),
                                      p3[:, dk, c * 128:(c + 1) * 128],
                                      r3[:, dk, :], start=st, stop=False))
            bu_mark = s_pe.inc(1)

            if coef_state.get("hcast"):
                wait("gp", s_pe, bu_mark)
                for _ in range(2):
                    nm, dstS, srcF, og = coef_state["hcast"].pop(0)
                    emit("gp", lambda e, d=dstS[og], s=srcF[og]:
                         e.dma_start(d, s).then_inc(s_dgp.h, 16))
                    wait("gp", s_dgp, s_dgp.inc(16))
                    prep_mark[(nm, og)] = s_dgp.val

            wait("ve", s_pe, bu_mark)
            vfence()
            for s in range(NSUB):
                bre = bank_sub(pair, s)
                bim = bank_sub(pair + 1, s)
                for i, (arr3, b) in enumerate(((A_re3, bre), (A_im3, bim),
                                               (A_im3, bre), (A_re3, bim))):
                    q = (s * 4 + i) * SUB
                    emit("ve", lambda e, c=c, a3=arr3, b=b, q=q:
                         e.tensor_tensor(tbig[:, q:q + SUB], a3[:, c, :], b,
                                         OP.mult))
            vfence()
            bankrel = s_ve.val
            for s in range(NSUB):
                q = s * 4 * SUB
                emit("ve", lambda e, q=q, d=2048 + s * SUB:
                     e.tensor_tensor(tbig[:, d:d + SUB], tbig[:, q:q + SUB],
                                     tbig[:, q + SUB:q + 2 * SUB],
                                     OP.subtract))
                emit("ve", lambda e, q=q, d=2560 + s * SUB:
                     e.tensor_tensor(tbig[:, d:d + SUB],
                                     tbig[:, q + 2 * SUB:q + 3 * SUB],
                                     tbig[:, q + 3 * SUB:q + 4 * SUB],
                                     OP.add))
            vfence()
            for s in range(NSUB):
                cs = slice(s * SUB, (s + 1) * SUB)
                emit("ve", lambda e, c=c, cs=cs, d=2048 + s * SUB:
                     e.tensor_tensor_scan(
                         S_re[:, cs], ones_t[:, :], tbig[:, d:d + SUB],
                         coef[:, C_IRE + c:C_IRE + c + 1], OP.mult, OP.add))
                emit("ve", lambda e, c=c, cs=cs, d=2560 + s * SUB:
                     e.tensor_tensor_scan(
                         S_im[:, cs], ones_t[:, :], tbig[:, d:d + SUB],
                         coef[:, C_IIM + c:C_IIM + c + 1], OP.mult, OP.add))
                vfence()
                send = s * SUB + SUB - 1
                emit("ve", lambda e, c=c, send=send: e.tensor_tensor(
                    coef[:, C_Q1:C_Q1 + 1],
                    coef[:, C_WRE + c:C_WRE + c + 1],
                    S_re[:, send:send + 1], OP.mult))
                emit("ve", lambda e, c=c, send=send: e.tensor_tensor(
                    coef[:, C_Q2:C_Q2 + 1],
                    coef[:, C_WIM + c:C_WIM + c + 1],
                    S_im[:, send:send + 1], OP.mult))
                emit("ve", lambda e, c=c, send=send: e.tensor_tensor(
                    coef[:, C_Q3:C_Q3 + 1],
                    coef[:, C_WRE + c:C_WRE + c + 1],
                    S_im[:, send:send + 1], OP.mult))
                emit("ve", lambda e, c=c, send=send: e.tensor_tensor(
                    coef[:, C_Q4:C_Q4 + 1],
                    coef[:, C_WIM + c:C_WIM + c + 1],
                    S_re[:, send:send + 1], OP.mult))
                vfence()
                emit("ve", lambda e, c=c: e.tensor_tensor(
                    coef[:, C_IRE + c:C_IRE + c + 1],
                    coef[:, C_Q1:C_Q1 + 1], coef[:, C_Q2:C_Q2 + 1],
                    OP.subtract))
                emit("ve", lambda e, c=c: e.tensor_tensor(
                    coef[:, C_IIM + c:C_IIM + c + 1],
                    coef[:, C_Q3:C_Q3 + 1], coef[:, C_Q4:C_Q4 + 1], OP.add))
                vfence()
            for s in range(NSUB):
                cs = slice(s * SUB, (s + 1) * SUB)
                for i, (r3, Sb) in enumerate(((Rn_im3, S_im), (Rn_re3, S_re),
                                              (Rn_re3, S_im), (Rn_im3, S_re))):
                    q = (s * 4 + i) * SUB
                    emit("ve", lambda e, c=c, cs=cs, r3=r3, Sb=Sb, q=q:
                         e.tensor_tensor(tbig[:, q:q + SUB], r3[:, c, :],
                                         Sb[:, cs], OP.mult))
            vfence()
            for s in range(NSUB):
                q = s * 4 * SUB
                emit("ve", lambda e, q=q, d=2048 + s * SUB:
                     e.tensor_tensor(tbig[:, d:d + SUB], tbig[:, q:q + SUB],
                                     tbig[:, q + SUB:q + 2 * SUB],
                                     OP.subtract))
                emit("ve", lambda e, q=q, d=2560 + s * SUB:
                     e.tensor_tensor(tbig[:, d:d + SUB],
                                     tbig[:, q + 2 * SUB:q + 3 * SUB],
                                     tbig[:, q + 3 * SUB:q + 4 * SUB],
                                     OP.add))
            vfence()
            for s in range(NSUB):
                cs = slice(s * SUB, (s + 1) * SUB)
                emit("ve", lambda e, c=c, cs=cs, d=2048 + s * SUB:
                     e.tensor_copy(hsrh3[:, c, cs], tbig[:, d:d + SUB]))
                emit("ve", lambda e, c=c, cs=cs, d=2560 + s * SUB:
                     e.tensor_copy(hsmh3[:, c, cs], tbig[:, d:d + SUB]))
            vfence()
            for s in range(NSUB):
                cs = slice(s * SUB, (s + 1) * SUB)
                emit("ve", lambda e, c=c, cs=cs, d=2048 + s * SUB:
                     e.tensor_tensor(hsrl3[:, c, cs], tbig[:, d:d + SUB],
                                     hsrh3[:, c, cs], OP.subtract))
                last = s == NSUB - 1
                if last:
                    emit("ve", lambda e, c=c, cs=cs, d=2560 + s * SUB:
                         e.tensor_tensor(hsml3[:, c, cs], tbig[:, d:d + SUB],
                                         hsmh3[:, c, cs], OP.subtract)
                         .then_inc(s_ve.h, 1))
                else:
                    emit("ve", lambda e, c=c, cs=cs, d=2560 + s * SUB:
                         e.tensor_tensor(hsml3[:, c, cs], tbig[:, d:d + SUB],
                                         hsmh3[:, c, cs], OP.subtract))
            hs_mark = s_ve.inc(1)
            hs_marks.append(hs_mark)
            bank_free[pair] = (s_ve, bankrel)
            bank_free[pair + 1] = (s_ve, bankrel)

        claim_banks(list(range(8)), extra=[(s_dsy, chi_mark)])
        for c in range(DC):
            wait("pe", s_ve, hs_marks[c])
            for dd in range(DC):
                st = (c == 0)
                emit("pe", lambda e, c=c, dd=dd, st=st: e.matmul(
                    bank(dd), cp_re3[:, c, dd * 128:(dd + 1) * 128],
                    hsrh3[:, c, :], start=st, stop=False))
                emit("pe", lambda e, c=c, dd=dd: e.matmul(
                    bank(dd), cp_im3[:, c, dd * 128:(dd + 1) * 128],
                    hsmh3[:, c, :], start=False, stop=False))
                emit("pe", lambda e, c=c, dd=dd: e.matmul(
                    bank(dd), cp_re3[:, c, dd * 128:(dd + 1) * 128],
                    hsrl3[:, c, :], start=False, stop=False))
                inc_here = (c == DC - 1 and dd == DC - 1)
                if inc_here:
                    emit("pe", lambda e, c=c, dd=dd: e.matmul(
                        bank(dd), cp_im3[:, c, dd * 128:(dd + 1) * 128],
                        hsml3[:, c, :], start=False, stop=False)
                         .then_inc(s_pe.h, 1))
                else:
                    emit("pe", lambda e, c=c, dd=dd: e.matmul(
                        bank(dd), cp_im3[:, c, dd * 128:(dd + 1) * 128],
                        hsml3[:, c, :], start=False, stop=False))
        pass2_mark = s_pe.inc(1)
        waits("ac", [(s_pe, pass2_mark), (s_bco, prep_mark["clo"][0]),
                     (s_bco2, prep_mark["clo"][1]), (s_dac, s_dac.val)])
        emit("ac", lambda e: e.dma_start(cp_re[:, :], clre)
             .then_inc(s_dac.h, 16))
        clre_mark = s_dac.inc(16)
        waits("sy", [(s_pe, pass2_mark), (s_bco, prep_mark["clo"][0]),
                     (s_bco2, prep_mark["clo"][1]), (s_dsy, s_dsy.val)])
        emit("sy", lambda e: e.dma_start(cp_im[:, :], clim)
             .then_inc(s_dsy.h, 16))
        clim_mark = s_dsy.inc(16)
        wait("pe", s_dac, clre_mark)
        for c in range(DC):
            for dd in range(DC):
                emit("pe", lambda e, c=c, dd=dd: e.matmul(
                    bank(dd), cp_re3[:, c, dd * 128:(dd + 1) * 128],
                    hsrh3[:, c, :], start=False, stop=False))
        wait("pe", s_dsy, clim_mark)
        for c in range(DC):
            for dd in range(DC):
                sp = (c == DC - 1)
                inc_here = sp and dd == DC - 1
                if inc_here:
                    emit("pe", lambda e, c=c, dd=dd: e.matmul(
                        bank(dd), cp_im3[:, c, dd * 128:(dd + 1) * 128],
                        hsmh3[:, c, :], start=False, stop=True)
                         .then_inc(s_pe.h, 1))
                else:
                    emit("pe", lambda e, c=c, dd=dd, sp=sp: e.matmul(
                        bank(dd), cp_im3[:, c, dd * 128:(dd + 1) * 128],
                        hsmh3[:, c, :], start=False, stop=sp))
        att_mark = s_pe.inc(1)

        wait("ve", s_pe, att_mark)
        for dd in range(DC):
            emit("ve", lambda e, dd=dd: e.tensor_tensor(
                x3[:, dd, :], x3[:, dd, :], bank(dd), OP.add))
        vfence()
        for dd in range(DC):
            emit("ve", lambda e, dd=dd: e.scalar_tensor_tensor(
                x3[:, dd, :], xn3[:, dd, :],
                vecs[:, V_DSK + dd:V_DSK + dd + 1], x3[:, dd, :],
                OP.mult, OP.add))
        vfence()
        for dd in range(DC):
            if dd == DC - 1:
                emit("ve", lambda e, dd=dd: e.scalar_tensor_tensor(
                    x3[:, dd, :], xl3[:, dd, :],
                    vecs[:, V_DSK + dd:V_DSK + dd + 1], x3[:, dd, :],
                    OP.mult, OP.add).then_inc(s_ve.h, 1))
            else:
                emit("ve", lambda e, dd=dd: e.scalar_tensor_tensor(
                    x3[:, dd, :], xl3[:, dd, :],
                    vecs[:, V_DSK + dd:V_DSK + dd + 1], x3[:, dd, :],
                    OP.mult, OP.add))
        final_mark = s_ve.inc(1)
        for bk in range(8):
            bank_free[bk] = (s_ve, final_mark)
        last_pe = s_pe.val
        wfree[0] = (s_pe, last_pe)
        wfree[1] = (s_pe, last_pe)
        return (s_ve, final_mark), last_pe

    coef_state["scal"] = emit_coef_scalars()

    x_dep = [(s_dsy, x_mark)]
    x_dep_h0 = None
    for tile in range(NT):
        ln1 = emit_ln(x_dep, produce_lo=False, x_dep_h0=x_dep_h0)
        xm, ffn_pe, _m5h, m4_pe = emit_ffn(ln1, tile)
        x_dep = [xm]
        x_half_marks = _m5h
        if phase_limit >= 2:
            ln2 = emit_ln(x_dep, produce_lo=True)
            am, att_pe = emit_attn(tile, ln2, ffn_pe, m4_pe)
            x_dep = [am]
        if phase_limit >= 3:
            ln3 = emit_ln(x_dep, produce_lo=False)
            xm2, _, _m5h, _ = emit_ffn(ln3, tile)
            x_dep = [xm2]
            x_half_marks = _m5h
        st_marks = []
        for h in range(2):
            waits("ac", [x_half_marks[h], (s_st, s_st.val)])
            emit("ac", lambda e, t=tile, h=h: e.dma_start(
                outT3[:, 4 * h:4 * h + 4, t * T:(t + 1) * T],
                x3[:, 4 * h:4 * h + 4, :]).then_inc(s_st.h, 16))
            st_marks.append(s_st.inc(16))
        if tile + 1 < NT:
            for h in range(2):
                waits("gp", [(s_st, st_marks[h]), (s_ld, s_ld.val)])
                emit("gp", lambda e, t=tile + 1, h=h: e.dma_start(
                    x3[:, 4 * h:4 * h + 4, :],
                    xT3[:, 4 * h:4 * h + 4, t * T:(t + 1) * T])
                     .then_inc(s_ld.h, 16))
                s_ld.inc(16)
            x_dep = [(s_ld, s_ld.val)]
            x_dep_h0 = [(s_ld, s_ld.val - 16)]

    wait("sy", s_dsy, s_dsy.val)
    wait("sy", s_st, s_st.val)
    wait("sy", s_dac, s_dac.val)
    wait("sy", s_ld, s_ld.val)
    wait("sy", s_wA, s_wA.val)
    wait("sy", s_wB, s_wB.val)
    wait("sy", s_gA, s_gA.val)
    wait("sy", s_gB, s_gB.val)
    wait("sy", s_bco, s_bco.val)
    wait("sy", s_bco2, s_bco2.val)
    wait("sy", s_dgp, s_dgp.val)

    with nc.Block() as block:
        @block.tensor
        def _(e):
            for f in Q["pe"]:
                f(e)

        @block.vector
        def _(e):
            for f in Q["ve"]:
                f(e)

        @block.scalar
        def _(e):
            for f in Q["ac"]:
                f(e)

        @block.gpsimd
        def _(e):
            for f in Q["gp"]:
                f(e)

        @block.sync
        def _(e):
            for f in Q["sy"]:
                f(e)

    ctx.close()
    return nc


_NC_CACHE = {}


def kernel(L0_attn_inputs, L0_attn_nu_log, L0_attn_theta_log, L0_attn_B_re,
           L0_attn_B_im, L0_attn_C_re, L0_attn_C_im, L0_attn_D_skip,
           L0_ffn_w1_w, L0_ffn_w1_s, L0_ffn_w2_w, L0_ffn_w2_s, L0_ffn_h1_w,
           L0_ffn_h1_s, L0_ffn_h2_w, L0_ffn_h2_s, L0_ffn_w3_w, L0_ffn_w3_s):
    f32 = np.float32

    def t(a):
        return np.ascontiguousarray(np.asarray(a, dtype=f32).T)

    def pc(v, c):
        return np.ascontiguousarray(np.asarray(v, dtype=f32).reshape(c, 128).T)

    def wpanel(a, nog, kc):
        a = np.asarray(a, dtype=f32)
        return np.ascontiguousarray(
            a.reshape(nog, 512, kc, 128).transpose(0, 3, 2, 1)
            .reshape(nog, 128, kc * 512))

    def bcpanel(a):
        a = np.asarray(a, dtype=f32)
        return np.ascontiguousarray(
            a.T.reshape(DC, 128, a.shape[0]).transpose(1, 0, 2)
            .reshape(128, DC * a.shape[0]))

    shared = {
        "w1T": wpanel(L0_ffn_w1_w, 8, DC), "w2T": wpanel(L0_ffn_w2_w, 8, DC),
        "h1T": wpanel(L0_ffn_h1_w, 8, HC), "h2T": wpanel(L0_ffn_h2_w, 8, HC),
        "w3T": wpanel(L0_ffn_w3_w, 2, HC),
        "breT": bcpanel(L0_attn_B_re), "bimT": bcpanel(L0_attn_B_im),
        "creT": bcpanel(L0_attn_C_re), "cimT": bcpanel(L0_attn_C_im),
        "s1pc": pc(L0_ffn_w1_s, HC), "s2pc": pc(L0_ffn_w2_s, HC),
        "h1spc": pc(L0_ffn_h1_s, HC), "h2spc": pc(L0_ffn_h2_s, HC),
        "w3spc": pc(L0_ffn_w3_s, DC), "dskpc": pc(L0_attn_D_skip, DC),
        "nulpc": pc(L0_attn_nu_log, DC), "thlpc": pc(L0_attn_theta_log, DC),
    }
    x = np.asarray(L0_attn_inputs, dtype=f32)
    in_maps = []
    for b in range(NCORES):
        m = dict(shared)
        m["xT"] = np.ascontiguousarray(x[b].T)
        in_maps.append(m)

    if "nc" not in _NC_CACHE:
        _NC_CACHE["nc"] = build()
    res = run_bass_kernel_spmd(_NC_CACHE["nc"], in_maps,
                               core_ids=list(range(NCORES)))
    out = np.stack([np.ascontiguousarray(r["outT"].T) for r in res.results])
    return out.astype(np.float32)
